# version 01
# baseline (speedup 1.0000x reference)
import contextlib
import numpy as np
import concourse.bass as bass
import concourse.mybir as mybir
from concourse.bass import ds
from concourse.bass_utils import run_bass_kernel_spmd

F32 = mybir.dt.float32
BF16 = mybir.dt.bfloat16
I32 = mybir.dt.int32
AF = mybir.ActivationFunctionType
ALU = mybir.AluOpType
AX = mybir.AxisListType
ET = mybir.EngineType

D = 4096
T = 2048
NCH = 32
NQ = 4
QW = 512
H = 32
HD = 128
INV_S128 = float(1.0 / np.sqrt(128.0))
THR = float(0.001 * np.sqrt(128.0))
CB = 6.0
EPS_LN = 1e-5
EPS_RMS = 1.1920929e-07
SINK_ITERS = 50
INV_TAU = 10.0
N_CORE = 8

_DEBUG = False
_ATTNA_TAP = False
_FFN_TAP = False
_ATTNA_SUB = 99
_LAST = {}
_CACHE = {}


class Sem:
    def __init__(self, nc, es, name):
        self.h = nc.semaphore(name)
        self.h = es.enter_context(self.h)
        self.c = 0

    def tick(self, inst, k=1):
        inst.then_inc(self.h, k)
        self.c += k
        return self.c


class B:
    def __init__(self):
        nc = bass.Bass("TRN2", target_bir_lowering=False)
        self.nc = nc
        es = contextlib.ExitStack()
        self.es = es

        def din(name, shape, dt=F32):
            return nc.dram_tensor(name, shape, dt, kind="ExternalInput").ap()

        self.xT = din("xT", [D, T])
        self.waq = din("waq", [D, D]); self.wak = din("wak", [D, D])
        self.wav = din("wav", [D, D]); self.wao = din("wao", [D, D])
        self.wbq = din("wbq", [D, D]); self.wbk = din("wbk", [D, D])
        self.wbv = din("wbv", [D, D]); self.wbo = din("wbo", [D, D])
        self.wcat = din("wcat", [D, 24])
        self.wbr = din("wbr", [1024, 1024])
        self.b_aq = din("b_aq", [128, 32]); self.b_ak = din("b_ak", [128, 32])
        self.b_av = din("b_av", [128, 32]); self.b_ao = din("b_ao", [128, 32])
        self.hbase = din("hbase", [24, 1])
        self.gate = din("gate", [1, 1])
        self.eye = din("eye", [128, 128])
        self.eye4 = din("eye4", [1, 16])
        self.nonce = din("nonce", [1, 8], I32)
        self.cid = din("cid", [1, 8], I32)

        self.OUT = nc.dram_tensor("OUT", [D, T], F32, kind="ExternalOutput").ap()

        kind = "ExternalOutput" if _DEBUG else "Internal"

        def dloc(name, shape, dt):
            return nc.dram_tensor(name, shape, dt, kind=kind).ap()

        self.SPQ = dloc("SPQ", [128, NCH, T], BF16)
        self.SPK = dloc("SPK", [128, NCH, T], BF16)
        self.SPV = dloc("SPV", [128, NCH, T], BF16)
        self.SPQ2 = dloc("SPQ2", [128, NCH, T], BF16)
        self.SPAV2 = dloc("SPAV2", [128, NCH, T], BF16)
        self.SPX1 = dloc("SPX1", [128, NCH, T], F32)
        self.SPX2 = dloc("SPX2", [128, NCH, T], F32)
        self.SPX3 = dloc("SPX3", [128, NCH, T], F32)
        self.DBGA = dloc("DBGA", [128, NCH, T], BF16)

        self.KT8 = nc.dram_tensor("KT8", [N_CORE, H, 128, T], BF16,
                                  kind="Internal", addr_space="Shared").ap()
        self.V8 = nc.dram_tensor("V8", [N_CORE, H, 16, 128, 128], BF16,
                                 kind="Internal", addr_space="Shared").ap()
        self.KTL = nc.dram_tensor("KTL", [H, 128, 2 * T], BF16,
                                  kind="Internal").ap()
        self.VL = nc.dram_tensor("VL", [H, 32, 128, 128], BF16,
                                 kind="Internal").ap()
        self.FLG = nc.dram_tensor("FLG", [N_CORE, 1, 8], I32,
                                  kind="Internal", addr_space="Shared").ap()

        AR_COLS = 97280
        self.AR = es.enter_context(nc.sbuf_tensor("AR", [128, AR_COLS], BF16))
        self.S0 = 65536

        self.ones_sq = es.enter_context(nc.sbuf_tensor("ones_sq", [128, 128], BF16))

        self.eye_bf = es.enter_context(nc.sbuf_tensor("eye_bf", [128, 128], BF16))
        self.eye_f = es.enter_context(nc.sbuf_tensor("eye_f", [128, 128], F32))
        self.ones_bf = es.enter_context(nc.sbuf_tensor("ones_bf", [128, 1], BF16))
        self.ones_f = es.enter_context(nc.sbuf_tensor("ones_f", [1, 128], F32))
        self.ba_s = es.enter_context(nc.sbuf_tensor("ba_s", [128, 128], F32))
        self.hb_s = es.enter_context(nc.sbuf_tensor("hb_s", [24, 1], F32))
        self.wcat_s = es.enter_context(nc.sbuf_tensor("wcat_s", [128, 32, 24], BF16))
        self.cst = es.enter_context(nc.sbuf_tensor("cst", [128, 4], F32))
        self.g_b = es.enter_context(nc.sbuf_tensor("g_b", [128, 2], F32))
        self.eye4b = es.enter_context(nc.sbuf_tensor("eye4b", [128, 16], F32))
        self.gt_s = es.enter_context(nc.sbuf_tensor("gt_s", [1, 2], F32))
        self.e4_s = es.enter_context(nc.sbuf_tensor("e4_s", [1, 16], F32))
        self.nz_s = es.enter_context(nc.sbuf_tensor("nz_s", [1, 8], I32))
        self.cid_s = es.enter_context(nc.sbuf_tensor("cid_s", [1, 8], I32))
        self.fr_s = es.enter_context(nc.sbuf_tensor("fr_s", [1, 8], I32))
        self.Zres = es.enter_context(nc.sbuf_tensor("Zres", [128, 16, 16], F32))
        self.Zpre = es.enter_context(nc.sbuf_tensor("Zpre", [128, 16, 4], F32))
        self.Zpost = es.enter_context(nc.sbuf_tensor("Zpost", [128, 16, 4], F32))
        self.Ebuf = es.enter_context(nc.sbuf_tensor("Ebuf", [128, 16, 16], F32))
        self.lnSu = es.enter_context(nc.sbuf_tensor("lnSu", [128, 16, 4], F32))
        self.lnSv = es.enter_context(nc.sbuf_tensor("lnSv", [128, 16, 4], F32))
        self.Sred = es.enter_context(nc.sbuf_tensor("Sred", [128, 16, 4], F32))
        self.Mred = es.enter_context(nc.sbuf_tensor("Mred", [128, 16, 4], F32))
        self.Hpre = es.enter_context(nc.sbuf_tensor("Hpre", [128, 16, 4], F32))
        self.Hpost = es.enter_context(nc.sbuf_tensor("Hpost", [128, 16, 4], F32))
        self.Hres = es.enter_context(nc.sbuf_tensor("Hres", [128, 16, 16], F32))
        self.sm4 = es.enter_context(nc.sbuf_tensor("sm4", [128, 16, 1], F32))

        self.PSA = es.enter_context(nc.psum_tensor("PSA", [128, 2048], F32))
        self.PSB = es.enter_context(nc.psum_tensor("PSB", [128, 2048], F32))

        self.pe = Sem(nc, es, "pe")
        self.ve = Sem(nc, es, "ve")
        self.se = Sem(nc, es, "se")
        self.gd = Sem(nc, es, "gd")
        self.sd = Sem(nc, es, "sd")
        self.sp = Sem(nc, es, "sp")
        self.bar = {e: Sem(nc, es, f"bar_{e}") for e in
                    ["tensor", "vector", "scalar", "gpsimd", "sync"]}
        self.nbar = 0

        self.blk = es.enter_context(nc.Block())

    def A(self, i=None, q=None):
        v = self.AR[:, 0:65536].rearrange("p (i t) -> p i t", i=NCH)
        if i is None:
            return v
        if q is None:
            return v[:, i, :]
        return v[:, i, q * QW:(q + 1) * QW]

    def S(self, off, cols, dtype=BF16, parts=128):
        v = self.AR[0:parts, self.S0 + off:self.S0 + off + cols]
        if dtype == F32:
            assert off % 2 == 0 and cols % 2 == 0
            v = v.bitcast(F32)
        return v

    def bank(self, ps, j):
        return ps[:, j * 512:(j + 1) * 512]

    def pid_of(self, eng):
        key = id(eng)
        if not hasattr(self, "_pidcache"):
            self._pidcache = {}
        if key not in self._pidcache:
            v = eng.value_load(self.cid_s[0:1, 0:1])
            self._pidcache[key] = self.nc.s_assert_within(
                v, 0, 7, skip_runtime_assert=True)
        return self._pidcache[key]

    def barrier(self):
        self.nbar += 1
        snap = [(self.pe, self.pe.c), (self.ve, self.ve.c), (self.se, self.se.c),
                (self.gd, self.gd.c), (self.sd, self.sd.c)]
        blk = self.blk
        for ename in ["tensor", "vector", "scalar", "gpsimd", "sync"]:
            def mk():
                def f(e):
                    for sem, val in snap:
                        if val > 0:
                            e.wait_ge(sem.h, val)
                return f
            getattr(blk, ename)(mk())

    def ph_init(self):
        blk = self.blk

        @blk.gpsimd
        def _(g):
            self.gd.tick(g.dma_start(out=self.eye_bf[:], in_=self.eye), 16)
            self.gd.tick(g.dma_start(
                out=self.wcat_s[:],
                in_=self.wcat.rearrange("(i p) n -> p i n", p=128)), 16)
            xsrc = self.xT.rearrange("(i p) t -> p i t", p=128)
            for kk in range(4):
                self.gd.tick(g.dma_start(
                    out=self.A()[:, kk * 8:(kk + 1) * 8, :],
                    in_=xsrc[:, kk * 8:(kk + 1) * 8, :]), 16)

        @blk.sync
        def _(sy):
            self.sd.tick(sy.dma_start(out=self.eye_f[:], in_=self.eye), 16)
            self.sd.tick(sy.dma_start(out=self.ba_s[:, 0:32], in_=self.b_aq), 16)
            self.sd.tick(sy.dma_start(out=self.ba_s[:, 32:64], in_=self.b_ak), 16)
            self.sd.tick(sy.dma_start(out=self.ba_s[:, 64:96], in_=self.b_av), 16)
            self.sd.tick(sy.dma_start(out=self.ba_s[:, 96:128], in_=self.b_ao), 16)
            self.sd.tick(sy.dma_start(out=self.hb_s[:], in_=self.hbase), 16)
            self.sd.tick(sy.dma_start(out=self.gt_s[0:1, 0:1], in_=self.gate), 16)
            self.sd.tick(sy.dma_start(out=self.nz_s[:], in_=self.nonce), 16)
            self.sd.tick(sy.dma_start(out=self.cid_s[:], in_=self.cid), 16)
            self.sd.tick(sy.dma_start(out=self.e4_s[:], in_=self.eye4), 16)

        @blk.vector
        def _(v):
            v.memset(self.ones_bf[:], 1.0)
            v.memset(self.ones_f[:], 1.0)
            v.memset(self.cst[:, 0:1], INV_S128)
            v.memset(self.cst[:, 1:2], -CB)
            v.memset(self.cst[:, 2:3], -1.0)
            v.memset(self.cst[:, 3:4], -0.5)
            self.ve.tick(v.memset(self.ones_sq[:], 1.0))

        self.barrier()
        b_se = self.se.c

        @blk.scalar
        def _(s):
            self.se.tick(s.activation(self.gt_s[0:1, 1:2], self.gt_s[0:1, 0:1],
                                      AF.Sigmoid))

        b_pe = self.pe.c

        @blk.tensor
        def _(t):
            t.wait_ge(self.se.h, b_se + 1)
            t.matmul(self.bank(self.PSA, 0)[:, 0:1], self.ones_f[:],
                     self.gt_s[0:1, 1:2], start=True, stop=True)
            self.pe.tick(t.matmul(self.bank(self.PSA, 1)[:, 0:16], self.ones_f[:],
                                  self.e4_s[:], start=True, stop=True))

        @blk.vector
        def _(v):
            v.wait_ge(self.pe.h, b_pe + 1)
            v.tensor_copy(self.g_b[:, 0:1], self.bank(self.PSA, 0)[:, 0:1])
            v.tensor_copy(self.eye4b[:], self.bank(self.PSA, 1)[:, 0:16])
            self.ve.tick(v.tensor_scalar(self.g_b[:, 1:2], self.g_b[:, 0:1],
                                         -1.0, 1.0, ALU.mult, ALU.add))

        self.barrier()

    def ph_stats(self, src, need_rho):
        blk = self.blk
        sq0 = self.S(8192, 2048)
        sq1 = self.S(10240, 2048)
        cs0 = self.S(12288, 2048)
        cs1 = self.S(14336, 2048)
        b_pe, b_se, b_gd = self.pe.c, self.se.c, self.gd.c

        if src is not None:
            @blk.gpsimd
            def _(g):
                for i in range(NCH):
                    if i >= 2:
                        g.wait_ge(self.pe.h, b_pe + (i - 1))
                    self.gd.tick(g.dma_start(
                        out=(cs0 if i % 2 == 0 else cs1), in_=src[:, i, :]), 16)

        @blk.scalar
        def _(s):
            for i in range(NCH):
                if src is not None:
                    s.wait_ge(self.gd.h, b_gd + 16 * (i + 1))
                    xin = cs0 if i % 2 == 0 else cs1
                else:
                    xin = self.A(i)
                if i >= 2:
                    s.wait_ge(self.pe.h, b_pe + (i - 1))
                self.se.tick(s.activation(sq0 if i % 2 == 0 else sq1, xin,
                                          AF.Square))

        @blk.tensor
        def _(t):
            for i in range(NCH):
                t.wait_ge(self.se.h, b_se + (i + 1))
                if src is not None:
                    xin = cs0 if i % 2 == 0 else cs1
                else:
                    xin = self.A(i)
                sq = sq0 if i % 2 == 0 else sq1
                last = None
                for q in range(NQ):
                    t.matmul(self.bank(self.PSA, q), self.ones_sq[:],
                             xin[:, q * QW:(q + 1) * QW],
                             start=(i == 0), stop=(i == NCH - 1))
                    last = t.matmul(self.bank(self.PSB, q),
                                    self.ones_sq[:], sq[:, q * QW:(q + 1) * QW],
                                    start=(i == 0), stop=(i == NCH - 1))
                self.pe.tick(last)

        self.barrier()
        mb = self.S(0, 4096, F32)
        sb = self.S(4096, 4096, F32)
        rho_b = self.S(16384, 4096, F32)
        dstf = self.S(8192, 4096, F32)
        vsc = self.S(12288, 4096, F32)
        b_ve1 = self.ve.c
        b_se1 = self.se.c

        @blk.vector
        def _(v):
            v.tensor_scalar_mul(mb, self.PSA[:, :], 1.0 / D)
            v.tensor_scalar_mul(dstf, self.PSB[:, :], 1.0 / D)
            v.tensor_mul(vsc, mb, mb)
            v.tensor_sub(dstf, dstf, vsc)
            self.ve.tick(v.tensor_scalar_add(vsc, dstf, EPS_LN))

        @blk.scalar
        def _(s):
            s.wait_ge(self.ve.h, b_ve1 + 1)
            s.activation(vsc, vsc, AF.Ln)
            self.se.tick(s.activation(sb, vsc, AF.Exp,
                                      scale=self.cst[:, 3:4]))
            if need_rho:
                self.se.tick(s.activation(vsc, vsc, AF.Exp,
                                          scale=self.cst[:, 2:3]))

        if need_rho:
            b_se2 = self.se.c
            b_ve2 = self.ve.c

            @blk.vector
            def _(v):
                v.wait_ge(self.se.h, b_se2)
                v.tensor_mul(dstf, dstf, vsc)
                self.ve.tick(v.tensor_scalar_add(dstf, dstf, EPS_RMS))

            @blk.scalar
            def _(s):
                s.wait_ge(self.ve.h, b_ve2 + 1)
                s.activation(dstf, dstf, AF.Ln)
                self.se.tick(s.activation(rho_b, dstf, AF.Exp,
                                          scale=self.cst[:, 3:4]))

        self.barrier()

    def ph_apply_ln(self, src):
        blk = self.blk
        mb = self.S(0, 4096, F32)
        sb = self.S(4096, 4096, F32)
        dst_f = self.S(8192, 4096, F32)
        cs0 = self.S(12288, 2048)
        cs1 = self.S(14336, 2048)
        b_gd, b_ve = self.gd.c, self.ve.c

        if src is not None:
            @blk.gpsimd
            def _(g):
                for i in range(NCH):
                    if i >= 2:
                        g.wait_ge(self.ve.h, b_ve + 2 * (i - 1))
                    self.gd.tick(g.dma_start(
                        out=(cs0 if i % 2 == 0 else cs1), in_=src[:, i, :]), 16)

        @blk.vector
        def _(v):
            for i in range(NCH):
                if src is None:
                    xin = self.A(i)
                else:
                    v.wait_ge(self.gd.h, b_gd + 16 * (i + 1))
                    xin = cs0 if i % 2 == 0 else cs1
                v.tensor_sub(dst_f, xin, mb)
                self.ve.tick(v.tensor_mul(self.A(i), dst_f, sb), 2)

        self.barrier()

    def ph_proj_fm(self, wt, out_cb, bias_col=None):
        blk = self.blk
        wb = [self.S(0, 8192), self.S(8192, 8192)]
        est = [self.S(16384, 2048), self.S(18432, 2048)]
        b_gd, b_pe, b_se, b_sd = self.gd.c, self.pe.c, self.se.c, self.sd.c

        @blk.gpsimd
        def _(g):
            for oc in range(16):
                if oc >= 2:
                    g.wait_ge(self.se.h, b_se + 8 * (oc - 1))
                self.gd.tick(g.dma_start(
                    out=wb[oc % 2].rearrange("p (i n) -> p i n", i=NCH),
                    in_=wt[:, oc * 256:(oc + 1) * 256].rearrange(
                        "(i p) n -> p i n", p=128)), 16)

        @blk.tensor
        def _(t):
            for oc in range(16):
                t.wait_ge(self.gd.h, b_gd + 16 * (oc + 1))
                if oc >= 1:
                    t.wait_ge(self.se.h, b_se + 8 * (oc - 1) + 8)
                wv = wb[oc % 2].rearrange("p (i n) -> p i n", i=NCH)
                for sub in range(2):
                    ps = self.PSA if sub == 0 else self.PSB
                    for q in range(NQ):
                        last = None
                        for i in range(NCH):
                            last = t.matmul(
                                self.bank(ps, q),
                                wv[:, i, sub * 128:(sub + 1) * 128],
                                self.A(i, q),
                                start=(i == 0), stop=(i == NCH - 1))
                        self.pe.tick(last)

        @blk.scalar
        def _(s):
            for oc in range(16):
                for sub in range(2):
                    ps = self.PSA if sub == 0 else self.PSB
                    d128 = oc * 2 + sub
                    if oc >= 1:
                        s.wait_ge(self.sd.h, b_sd + 16 * (2 * (oc - 1) + sub + 1))
                    for q in range(NQ):
                        s.wait_ge(self.pe.h, b_pe + oc * 8 + sub * 4 + q + 1)
                        if bias_col is not None:
                            self.se.tick(s.activation(
                                est[sub][:, q * QW:(q + 1) * QW],
                                self.bank(ps, q), AF.Identity,
                                bias=bias_col(d128)))
                        else:
                            self.se.tick(s.activation(
                                est[sub][:, q * QW:(q + 1) * QW],
                                self.bank(ps, q), AF.Copy))

        @blk.sync
        def _(sy):
            ctx = {}
            for oc in range(16):
                for sub in range(2):
                    d128 = oc * 2 + sub
                    sy.wait_ge(self.se.h, b_se + d128 * 4 + 4)
                    out_cb(d128, est[sub], sy, ctx)

        self.barrier()

    def ph_proj_resid(self, wt, xold, xnew, bias_col=None):
        blk = self.blk
        wb = [self.S(0, 8192), self.S(8192, 8192)]
        ev = [self.S(16384 + 1024 * k, 1024, F32) for k in range(4)]
        xo = [self.S(20480 + 1024 * k, 1024, F32) for k in range(4)]
        b_gd, b_pe, b_se, b_sd, b_ve = (self.gd.c, self.pe.c, self.se.c,
                                        self.sd.c, self.ve.c)

        @blk.gpsimd
        def _(g):
            for oc in range(16):
                if oc >= 2:
                    g.wait_ge(self.se.h, b_se + 8 * (oc - 1))
                self.gd.tick(g.dma_start(
                    out=wb[oc % 2].rearrange("p (i n) -> p i n", i=NCH),
                    in_=wt[:, oc * 256:(oc + 1) * 256].rearrange(
                        "(i p) n -> p i n", p=128)), 16)

        @blk.tensor
        def _(t):
            for oc in range(16):
                t.wait_ge(self.gd.h, b_gd + 16 * (oc + 1))
                if oc >= 1:
                    t.wait_ge(self.se.h, b_se + 8 * (oc - 1) + 8)
                wv = wb[oc % 2].rearrange("p (i n) -> p i n", i=NCH)
                for sub in range(2):
                    ps = self.PSA if sub == 0 else self.PSB
                    for q in range(NQ):
                        last = None
                        for i in range(NCH):
                            last = t.matmul(
                                self.bank(ps, q),
                                wv[:, i, sub * 128:(sub + 1) * 128],
                                self.A(i, q),
                                start=(i == 0), stop=(i == NCH - 1))
                        self.pe.tick(last)

        @blk.scalar
        def _(s):
            for d128 in range(32):
                oc, sub = d128 // 2, d128 % 2
                ps = self.PSA if sub == 0 else self.PSB
                for q in range(NQ):
                    k = d128 * 4 + q
                    if k >= 4:
                        s.wait_ge(self.sd.h, b_sd + 16 * (2 * (k - 4) + 2))
                    s.wait_ge(self.pe.h, b_pe + oc * 8 + sub * 4 + q + 1)
                    if bias_col is not None:
                        self.se.tick(s.activation(ev[k % 4], self.bank(ps, q),
                                                  AF.Identity,
                                                  bias=bias_col(d128)))
                    else:
                        self.se.tick(s.activation(ev[k % 4], self.bank(ps, q),
                                                  AF.Copy))

        @blk.vector
        def _(v):
            for d128 in range(32):
                for q in range(NQ):
                    k = d128 * 4 + q
                    v.wait_ge(self.se.h, b_se + k + 1)
                    v.wait_ge(self.sd.h, b_sd + 16 * (2 * k + 1))
                    self.ve.tick(v.tensor_add(ev[k % 4], ev[k % 4], xo[k % 4]))

        @blk.sync
        def _(sy):
            for d128 in range(32):
                for q in range(NQ):
                    k = d128 * 4 + q
                    if k >= 4:
                        sy.wait_ge(self.ve.h, b_ve + (k - 4) + 1)
                    self.sd.tick(sy.dma_start(
                        out=xo[k % 4], in_=xold[:, d128, q * QW:(q + 1) * QW]), 16)
                    sy.wait_ge(self.ve.h, b_ve + k + 1)
                    self.sd.tick(sy.dma_start(
                        out=xnew[:, d128, q * QW:(q + 1) * QW], in_=ev[k % 4]), 16)

        self.barrier()

    def ph_attn_a(self):
        blk = self.blk
        TQ = 256
        qc = self.S(0, 8192)
        kc = self.S(8192, 8192)
        vc = self.S(16384, 8192)
        pm = self.S(24576, 512)
        pex = self.S(25088, 512)
        pbf = self.S(25600, 512)
        pT = self.S(26112, 512)
        sums = self.S(26624, 32, F32)
        rec = self.S(26688, 32, F32)
        rmx = self.S(26720, 32, F32)
        vt = self.S(26752, 2048)
        df = self.S(28800, 1024, F32)
        dfv = df.rearrange("p (a b) -> p a b", a=16)

        qv = qc.rearrange("p (i t) -> p i t", i=NCH)
        kv_ = kc.rearrange("p (i t) -> p i t", i=NCH)
        vv = vc.rearrange("p (i t) -> p i t", i=NCH)
        vtv = vt.rearrange("p (g e) -> p g e", g=16)

        for tc in range(8):
            b_gd = self.gd.c

            @blk.gpsimd
            def _(g, tc=tc):
                self.gd.tick(g.dma_start(
                    out=qc, in_=self.SPQ[:, :, tc * TQ:(tc + 1) * TQ]), 16)
                self.gd.tick(g.dma_start(
                    out=kc, in_=self.SPK[:, :, tc * TQ:(tc + 1) * TQ]), 16)
                self.gd.tick(g.dma_start(
                    out=vc, in_=self.SPV[:, :, tc * TQ:(tc + 1) * TQ]), 16)

            for g64 in range(4):
                sb_ = self.bank(self.PSA, g64 % 2)
                b_pe0 = self.pe.c
                b_se0 = self.se.c
                b_ve0 = self.ve.c

                def vtb_of(self, grp):
                    bnk = self.bank(self.PSA, 2 + grp % 2).bitcast(BF16)
                    off = ((grp // 2) % 4) * 128
                    return bnk[:, off:off + 128]

                @blk.tensor
                def _(t, tc=tc, g64=g64, sb_=sb_, b_gd=b_gd,
                      b_pe0=b_pe0, b_ve0=b_ve0):
                    t.wait_ge(self.gd.h, b_gd + 48)
                    last = None
                    for grp in range(16):
                        for j in range(4):
                            tok = g64 * 64 + grp * 4 + j
                            last = t.matmul(
                                sb_[j * 32:(j + 1) * 32,
                                    grp * 32:(grp + 1) * 32],
                                qv[:, :, tok], kv_[:, :, tok],
                                start=True, stop=True,
                                tile_position=(0, j * 32))
                    self.pe.tick(last)
                    for grp in range(16):
                        if grp >= 2:
                            t.wait_ge(self.ve.h, b_ve0 + 1 + (grp - 2) + 1)
                        last = None
                        for j in range(4):
                            tok = g64 * 64 + grp * 4 + j
                            last = t.transpose(
                                vtb_of(self, grp)[j * 32:(j + 1) * 32, :],
                                vv[:, :, tok], self.eye_bf[:],
                                tile_position=(0, j * 32))
                        self.pe.tick(last)

                @blk.vector
                def _(v, sb_=sb_, b_pe0=b_pe0, b_se0=b_se0):
                    sbv = sb_.rearrange("p (a b) -> p a b", a=16)
                    v.wait_ge(self.pe.h, b_pe0 + 1)
                    v.tensor_single_scalar(pm, sb_, THR, ALU.is_gt)
                    v.tensor_reduce(
                        rmx.rearrange("p (a b) -> p a b", a=16)[:, :, 0:1],
                        sbv, AX.X, ALU.max)
                    self.ve.tick(v.tensor_sub(
                        dfv, sbv,
                        rmx.rearrange("p (a b) -> p a b", a=16)[
                            :, :, 0:1].broadcast_to([128, 16, 32])))
                    for grp in range(16):
                        v.wait_ge(self.pe.h, b_pe0 + 2 + grp)
                        self.ve.tick(v.tensor_copy(vtv[:, grp, :],
                                                   vtb_of(self, grp)))
                    v.wait_ge(self.se.h, b_se0 + 1)
                    v.tensor_mul(pbf, pex, pm)
                    self.ve.tick(v.tensor_reduce(
                        sums.rearrange("p (a b) -> p a b", a=16)[:, :, 0:1],
                        pbf.rearrange("p (a b) -> p a b", a=16), AX.X, ALU.add))
                    v.wait_ge(self.se.h, b_se0 + 2)
                    v.tensor_mul(
                        pbf.rearrange("p (a b) -> p a b", a=16),
                        pbf.rearrange("p (a b) -> p a b", a=16),
                        rec.rearrange("p (a b) -> p a b", a=16)[
                            :, :, 0:1].broadcast_to([128, 16, 32]))
                    self.ve.tick(v.transpose(pT, pbf))

                @blk.scalar
                def _(s, b_ve0=b_ve0):
                    s.wait_ge(self.ve.h, b_ve0 + 1)
                    self.se.tick(s.activation(pex, df, AF.Exp,
                                              scale=self.cst[:, 0:1]))
                    s.wait_ge(self.ve.h, b_ve0 + 18)
                    s.activation(rec, sums, AF.Ln)
                    self.se.tick(s.activation(rec, rec, AF.Exp,
                                              scale=self.cst[:, 2:3]))

                b_ve1 = self.ve.c
                b_pe2 = self.pe.c

                @blk.tensor
                def _(t, b_ve1=b_ve1):
                    t.wait_ge(self.ve.h, b_ve1)
                    last = None
                    for grp in range(16):
                        for j in range(4):
                            last = t.matmul(
                                self.bank(self.PSB, j)[
                                    :, grp * 32:(grp + 1) * 32],
                                vtv[32 * j:32 * (j + 1), grp, :],
                                pT[32 * j:32 * (j + 1),
                                   grp * 32:(grp + 1) * 32],
                                start=True, stop=True,
                                tile_position=(j * 32, 0))
                    self.pe.tick(last)

                b_ve2 = self.ve.c

                @blk.vector
                def _(v, tc=tc, g64=g64, b_pe2=b_pe2):
                    v.wait_ge(self.pe.h, b_pe2 + 1)
                    t0 = tc * TQ + g64 * 64
                    for j in range(4):
                        ab = self.bank(self.PSB, j)
                        dst = self.A()[:, :, t0 + j:t0 + 64:4].transpose(
                            [0, 2, 1])
                        self.ve.tick(v.tensor_copy(
                            dst, ab.rearrange("p (a b) -> p a b", a=16)))

                if _ATTNA_TAP and tc == 0 and g64 == 0:
                    @blk.scalar
                    def _(s, sb_=sb_):
                        s.wait_ge(self.ve.h, self.ve.c)
                        self.se.tick(s.activation(
                            self.S(29824, 1024, F32)[:, 0:512], sb_, AF.Copy))

                    @blk.gpsimd
                    def _(g):
                        g.wait_ge(self.se.h, self.se.c)
                        self.gd.tick(g.dma_start(
                            out=self.DBGA[:, 6, 0:512],
                            in_=self.S(29824, 1024, F32)[:, 0:512]), 16)

                    @blk.sync
                    def _(sy):
                        sy.wait_ge(self.ve.h, self.ve.c)
                        self.sd.tick(sy.dma_start(out=self.DBGA[:, 0, 0:512], in_=pm), 16)
                        self.sd.tick(sy.dma_start(out=self.DBGA[:, 1, 0:512], in_=pex), 16)
                        self.sd.tick(sy.dma_start(out=self.DBGA[:, 2, 0:512], in_=pbf), 16)
                        self.sd.tick(sy.dma_start(out=self.DBGA[:, 3, 0:512], in_=pT), 16)
                        self.sd.tick(sy.dma_start(out=self.DBGA[:, 4, :], in_=vt), 16)

                    @blk.gpsimd
                    def _(g):
                        g.wait_ge(self.ve.h, self.ve.c)
                        self.gd.tick(g.dma_start(out=self.DBGA[:, 5, 0:16], in_=sums), 16)
                        self.gd.tick(g.dma_start(out=self.DBGA[:, 5, 16:32], in_=rec), 16)

                self.barrier()

        self.barrier()

    def ph_proj_v_shared(self):
        blk = self.blk
        wb = self.S(0, 16384)
        est = [self.S(16384, 512), self.S(16896, 512)]
        b_gd, b_pe, b_se, b_sd = self.gd.c, self.pe.c, self.se.c, self.sd.c

        @blk.gpsimd
        def _(g):
            for dc in range(8):
                if dc >= 1:
                    g.wait_ge(self.pe.h, b_pe + dc * 16)
                self.gd.tick(g.dma_start(
                    out=wb.rearrange("p (i n) -> p i n", i=NCH),
                    in_=self.wbv[:, dc * 512:(dc + 1) * 512].rearrange(
                        "(i p) n -> p i n", p=128)), 16)

        @blk.tensor
        def _(t):
            for dc in range(8):
                t.wait_ge(self.gd.h, b_gd + 16 * (dc + 1))
                wv = wb.rearrange("p (i n) -> p i n", i=NCH)
                for tc in range(16):
                    k = dc * 16 + tc
                    ps = self.bank(self.PSA if k % 2 == 0 else self.PSB, 0)
                    if k >= 2:
                        t.wait_ge(self.se.h, b_se + (k - 1))
                    last = None
                    for i in range(NCH):
                        last = t.matmul(
                            ps, self.A(i)[:, tc * 128:(tc + 1) * 128],
                            wv[:, i, :], start=(i == 0), stop=(i == NCH - 1))
                    self.pe.tick(last)

        @blk.scalar
        def _(s):
            for dc in range(8):
                for tc in range(16):
                    k = dc * 16 + tc
                    s.wait_ge(self.pe.h, b_pe + k + 1)
                    if k >= 2:
                        s.wait_ge(self.sd.h, b_sd + 16 * (k - 1))
                    ps = self.bank(self.PSA if k % 2 == 0 else self.PSB, 0)
                    self.se.tick(s.activation(est[k % 2], ps, AF.Copy))

        @blk.sync
        def _(sy):
            for dc in range(8):
                for tc in range(16):
                    k = dc * 16 + tc
                    sy.wait_ge(self.se.h, b_se + k + 1)
                    dst = self.VL[4 * dc:4 * dc + 4, tc, :, :]
                    self.sd.tick(sy.dma_start(
                        out=dst.transpose([1, 0, 2]),
                        in_=est[k % 2].rearrange("p (h e) -> p h e", h=4)), 16)

        self.barrier()

    def ph_barrier_kv(self):
        blk = self.blk
        nc = self.nc

        @blk.gpsimd
        def _(g):
            pid = self.pid_of(g)
            self.gd.tick(g.dma_start(out=self.KT8[ds(pid, 1)][0],
                                     in_=self.KTL[:, :, 0:T]), 16)
            self.gd.tick(g.dma_start(out=self.V8[ds(pid, 1)][0],
                                     in_=self.VL[:, 0:16]), 16)
            g.wait_ge(self.gd.h, self.gd.c)
            self.gd.tick(g.dma_start(out=self.FLG[ds(pid, 1)][0],
                                     in_=self.nz_s[:]), 16)
            g.wait_ge(self.gd.h, self.gd.c)
            nonce_v = g.value_load(self.nz_s[0:1, 0:1])
            partner = pid // 2 * 2 + (1 - pid % 2)
            cnt = nc.alloc_registers("spincnt", engines=[ET.Pool])
            nc.regs_mov(cnt, 0)
            nc.br("kvspin", engines=[ET.Pool])
            with nc.body("kvspin", valid_engines=[ET.Pool]):
                nc.regs_alu(cnt, cnt, 16, op=ALU.add)
                cval = nc.snap(cnt, engines=[ET.Pool], min_val=16)
                g.dma_start(out=self.fr_s[:],
                            in_=self.FLG[ds(partner, 1)][0]).then_inc(self.sp.h, 16)
                g.wait_ge(self.sp.h, cval)
                v = g.value_load(self.fr_s[0:1, 0:1])
                nc.br_cmp(v, nonce_v, "kvdone", "kvspin", "IS_EQ",
                          engines=[ET.Pool])
            with nc.body("kvdone", valid_engines=[ET.Pool]):
                partner2 = pid // 2 * 2 + (1 - pid % 2)
                g.dma_start(out=self.KTL[:, :, T:2 * T],
                            in_=self.KT8[ds(partner2, 1)][0]).then_inc(self.gd.h, 16)
                g.dma_start(out=self.VL[:, 16:32],
                            in_=self.V8[ds(partner2, 1)][0]).then_inc(self.gd.h, 16)
                g.wait_ge(self.gd.h, self.gd.c + 32)
            self.gd.c += 32

        self.barrier()

    def ph_reload_A(self, src):
        blk = self.blk

        @blk.gpsimd
        def _(g):
            for kk in range(4):
                self.gd.tick(g.dma_start(
                    out=self.A()[:, kk * 8:(kk + 1) * 8, :],
                    in_=src[:, kk * 8:(kk + 1) * 8, :]), 16)

        self.barrier()

    def ph_attn_b(self):
        blk = self.blk
        kb = [self.S(0, 4096), self.S(4096, 4096)]
        vbr = [self.S(8192, 4096), self.S(12288, 4096)]
        pb = [self.S(16384, 2048), self.S(18432, 2048)]
        rb = self.S(20480, 1024, F32)
        avs = [self.S(21504, 512), self.S(22016, 512)]

        b_gd, b_pe, b_se, b_ve, b_sd = (self.gd.c, self.pe.c, self.se.c,
                                        self.ve.c, self.sd.c)
        PEH, SEH, VEH = 16, 10, 1

        @blk.gpsimd
        def _(g):
            for h in range(H):
                if h >= 2:
                    g.wait_ge(self.pe.h, b_pe + (h - 1) * NQ * PEH)
                self.gd.tick(g.dma_start(
                    out=kb[h % 2], in_=self.KTL[h]), 16)
                vv = vbr[h % 2].rearrange("p (a e) -> p a e", a=32)
                self.gd.tick(g.dma_start(
                    out=vv, in_=self.VL[h].transpose([1, 0, 2])), 16)

        @blk.tensor
        def _(t):
            for h in range(H):
                t.wait_ge(self.gd.h, b_gd + 32 * (h + 1))
                vv = vbr[h % 2].rearrange("p (a e) -> p a e", a=32)
                for q in range(NQ):
                    hq = h * NQ + q
                    avb = self.bank(self.PSB, hq % 2)
                    smb = self.bank(self.PSB, 2 + hq % 2)
                    if hq >= 2:
                        t.wait_ge(self.ve.h, b_ve + (hq - 2) * VEH + 1)
                    for kg in range(8):
                        if kg > 0:
                            t.wait_ge(self.se.h, b_se + hq * SEH + kg)
                        elif hq > 0:
                            t.wait_ge(self.se.h, b_se + (hq - 1) * SEH + 8)
                        last = None
                        for kk in range(4):
                            kt = kg * 4 + kk
                            last = t.matmul(
                                self.bank(self.PSA, kk),
                                kb[h % 2][:, kt * 128:(kt + 1) * 128],
                                self.A(h, q), start=True, stop=True)
                        self.pe.tick(last)
                        t.wait_ge(self.se.h, b_se + hq * SEH + kg + 1)
                        last = None
                        for kk in range(4):
                            kt = kg * 4 + kk
                            pbv = pb[kg % 2][:, kk * 512:(kk + 1) * 512]
                            t.matmul(smb, self.ones_sq[:], pbv,
                                     start=(kt == 0), stop=(kt == 31))
                            last = t.matmul(avb, vv[:, kt, :], pbv,
                                            start=(kt == 0), stop=(kt == 31))
                        self.pe.tick(last)

        @blk.scalar
        def _(s):
            for h in range(H):
                for q in range(NQ):
                    hq = h * NQ + q
                    smb = self.bank(self.PSB, 2 + hq % 2)
                    for kg in range(8):
                        s.wait_ge(self.pe.h, b_pe + hq * PEH + 2 * kg + 1)
                        if kg >= 2:
                            s.wait_ge(self.pe.h,
                                      b_pe + hq * PEH + 2 * (kg - 2) + 2)
                        elif hq > 0:
                            s.wait_ge(self.pe.h,
                                      b_pe + (hq - 1) * PEH + 2 * (kg + 6) + 2)
                        self.se.tick(s.activation(
                            pb[kg % 2], self.PSA[:, :], AF.Exp,
                            scale=self.cst[:, 0:1], bias=self.cst[:, 1:2]))
                    s.wait_ge(self.pe.h, b_pe + hq * PEH + 16)
                    if hq > 0:
                        s.wait_ge(self.ve.h, b_ve + (hq - 1) * VEH + 1)
                    self.se.tick(s.activation(rb, smb, AF.Ln))
                    self.se.tick(s.activation(rb, rb, AF.Exp,
                                              scale=self.cst[:, 2:3]))

        @blk.vector
        def _(v):
            for h in range(H):
                for q in range(NQ):
                    hq = h * NQ + q
                    avb = self.bank(self.PSB, hq % 2)
                    v.wait_ge(self.se.h, b_se + hq * SEH + 10)
                    if hq >= 2:
                        v.wait_ge(self.sd.h, b_sd + 16 * (hq - 1))
                    self.ve.tick(v.tensor_mul(avs[hq % 2], avb, rb))

        @blk.sync
        def _(sy):
            for h in range(H):
                for q in range(NQ):
                    hq = h * NQ + q
                    sy.wait_ge(self.ve.h, b_ve + hq * VEH + 1)
                    self.sd.tick(sy.dma_start(
                        out=self.SPAV2[:, h, q * QW:(q + 1) * QW],
                        in_=avs[hq % 2]), 16)

        self.barrier()

    def ph_ffn(self, xold, xnew, fin=False):
        blk = self.blk
        Lrows = self.S(0, 4096, F32, parts=24)
        HrowsR = self.S(0, 2048, BF16, parts=16)
        HrowsPre = self.S(2048, 2048, BF16, parts=4)
        HrowsPost = self.S(4096, 2048, BF16, parts=4)
        rho_b = self.S(16384, 4096, F32)
        hbv = self.S(12288, 3072)
        wbc = [self.S(18432, 1024), self.S(19456, 1024)]
        xpre = self.S(20480, 4096)
        ybf = self.S(24576, 4096)
        accf = self.S(28672, 1024, F32)
        tmpf = self.S(29696, 1024, F32)
        xof = self.S(30720, 1024, F32)

        hv = hbv.rearrange("p (r t) -> p r t", r=6)
        xpv = xpre.rearrange("p (c t) -> p c t", c=8)
        yv = ybf.rearrange("p (c t) -> p c t", c=8)

        b_pe = self.pe.c

        @blk.tensor
        def _(t):
            for q in range(NQ):
                last = None
                for i in range(NCH):
                    last = t.matmul(self.bank(self.PSA, q)[0:24, :],
                                    self.wcat_s[:, i, :], self.A(i, q),
                                    start=(i == 0), stop=(i == NCH - 1))
                self.pe.tick(last)

        b_pe1 = self.pe.c

        @blk.vector
        def _(v):
            v.wait_ge(self.pe.h, b_pe1)
            for q in range(NQ):
                v.tensor_mul(Lrows[:, q * QW:(q + 1) * QW],
                             self.bank(self.PSA, q)[0:24, :],
                             rho_b[0:24, q * QW:(q + 1) * QW])
            self.ve.tick(v.tensor_scalar_add(Lrows, Lrows, self.hb_s[:]))

        self.barrier()
        if _FFN_TAP:
            @blk.gpsimd
            def _(g):
                self.gd.tick(g.dma_start(out=self.DBGA[0:24, 7, :],
                                         in_=Lrows), 16)
                self.gd.tick(g.dma_start(out=self.DBGA[:, 12, 0:2048],
                                         in_=rho_b), 16)
            self.barrier()

        b_pe2 = self.pe.c
        b_ve2 = self.ve.c

        @blk.tensor
        def _(t):
            for tt in range(16):
                if tt >= 2:
                    t.wait_ge(self.ve.h, b_ve2 + 3 * (tt - 1))
                self.pe.tick(t.transpose(
                    self.bank(self.PSA, tt % 2)[0:128, 0:24],
                    Lrows[:, tt * 128:(tt + 1) * 128],
                    self.eye_f[0:24, 0:24]))

        @blk.vector
        def _(v):
            for tt in range(16):
                v.wait_ge(self.pe.h, b_pe2 + tt + 1)
                ps = self.bank(self.PSA, tt % 2)
                v.tensor_scalar_mul(self.Zres[:, tt, :], ps[:, 0:16], INV_TAU)
                v.tensor_copy(self.Zpre[:, tt, :], ps[:, 16:20])
                self.ve.tick(v.tensor_copy(self.Zpost[:, tt, :],
                                           ps[:, 20:24]), 3)

        self.barrier()

        b_se1 = self.se.c

        @blk.scalar
        def _(s):
            self.se.tick(s.activation(self.Hpre[:], self.Zpre[:], AF.Exp))
            self.se.tick(s.activation(self.Hpost[:], self.Zpost[:], AF.Exp))

        b_ve1b = self.ve.c

        @blk.vector
        def _(v):
            for k, hh in enumerate([self.Hpre, self.Hpost]):
                v.wait_ge(self.se.h, b_se1 + k + 1)
                self.ve.tick(v.tensor_reduce(self.sm4[:], hh[:], AX.X, ALU.add))
                v.wait_ge(self.se.h, b_se1 + 2 + 2 * (k + 1))
                self.ve.tick(v.tensor_mul(
                    hh[:], hh[:], self.sm4[:].broadcast_to([128, 16, 4])))

        @blk.scalar
        def _(s):
            for k in range(2):
                s.wait_ge(self.ve.h, b_ve1b + 2 * k + 1)
                s.activation(self.sm4[:], self.sm4[:], AF.Ln)
                self.se.tick(s.activation(self.sm4[:], self.sm4[:], AF.Exp,
                                          scale=self.cst[:, 2:3]), 2)

        @blk.vector
        def _(v):
            self.ve.tick(v.memset(self.lnSv[:], 0.0))

        Zv = self.Zres[:].rearrange("p a (i j) -> p a i j", i=4)
        Ev = self.Ebuf[:].rearrange("p a (i j) -> p a i j", i=4)
        EvT = self.Ebuf[:].rearrange("p a (i j) -> p a j i", i=4)
        for it in range(SINK_ITERS):
            b_v = self.ve.c
            b_s = self.se.c

            @blk.vector
            def _(v):
                v.tensor_sub(Ev, Zv, self.lnSv[:, :, None, :].broadcast_to(
                    [128, 16, 4, 4]))
                v.tensor_reduce(self.Mred[:, :, :, None], Ev, AX.X, ALU.max)
                self.ve.tick(v.tensor_sub(
                    Ev, Ev, self.Mred[:, :, :, None].broadcast_to(
                        [128, 16, 4, 4])))

            @blk.scalar
            def _(s, b_v=b_v):
                s.wait_ge(self.ve.h, b_v + 1)
                self.se.tick(s.activation(self.Ebuf[:], self.Ebuf[:], AF.Exp))

            @blk.vector
            def _(v, b_s=b_s):
                v.wait_ge(self.se.h, b_s + 1)
                self.ve.tick(v.tensor_reduce(self.Sred[:, :, :, None], Ev,
                                             AX.X, ALU.add))

            @blk.scalar
            def _(s, b_v=b_v):
                s.wait_ge(self.ve.h, b_v + 2)
                self.se.tick(s.activation(self.lnSu[:], self.Sred[:], AF.Ln))

            @blk.vector
            def _(v, b_s=b_s):
                v.wait_ge(self.se.h, b_s + 2)
                v.tensor_add(self.lnSu[:], self.lnSu[:], self.Mred[:])
                v.tensor_sub(Ev, Zv, self.lnSu[:, :, :, None].broadcast_to(
                    [128, 16, 4, 4]))
                v.tensor_reduce(self.Mred[:, :, :, None], EvT, AX.X, ALU.max)
                self.ve.tick(v.tensor_sub(
                    Ev, Ev, self.Mred[:, :, None, :].broadcast_to(
                        [128, 16, 4, 4])))

            @blk.scalar
            def _(s, b_v=b_v):
                s.wait_ge(self.ve.h, b_v + 3)
                self.se.tick(s.activation(self.Ebuf[:], self.Ebuf[:], AF.Exp))

            @blk.vector
            def _(v, b_s=b_s):
                v.wait_ge(self.se.h, b_s + 3)
                self.ve.tick(v.tensor_reduce(self.Sred[:, :, :, None], EvT,
                                             AX.X, ALU.add))

            @blk.scalar
            def _(s, b_v=b_v):
                s.wait_ge(self.ve.h, b_v + 4)
                self.se.tick(s.activation(self.lnSv[:], self.Sred[:], AF.Ln))

            @blk.vector
            def _(v, b_s=b_s):
                v.wait_ge(self.se.h, b_s + 4)
                self.ve.tick(v.tensor_add(self.lnSv[:], self.lnSv[:],
                                          self.Mred[:]))

        b_s2 = self.se.c
        b_v2 = self.ve.c

        @blk.vector
        def _(v):
            v.wait_ge(self.se.h, b_s2)
            v.tensor_sub(Ev, Zv,
                         self.lnSu[:, :, :, None].broadcast_to([128, 16, 4, 4]))
            self.ve.tick(v.tensor_sub(
                Ev, Ev, self.lnSv[:, :, None, :].broadcast_to([128, 16, 4, 4])))

        @blk.scalar
        def _(s):
            s.wait_ge(self.ve.h, b_v2 + 1)
            self.se.tick(s.activation(self.Hres[:], self.Ebuf[:], AF.Exp))

        Hv = self.Hres[:].rearrange("p a (i j) -> p a i j", i=4)

        @blk.vector
        def _(v):
            v.wait_ge(self.se.h, b_s2 + 1)
            v.tensor_scalar_mul(self.Hres[:], self.Hres[:], self.g_b[:, 0:1])
            self.ve.tick(v.scalar_tensor_tensor(
                Hv,
                self.eye4b[:].rearrange("p (i j) -> p i j", i=4)[
                    :, None, :, :].broadcast_to([128, 16, 4, 4]),
                self.g_b[:, 1:2], Hv, ALU.mult, ALU.add))

        self.barrier()

        b_pe3 = self.pe.c
        b_ve3 = self.ve.c

        @blk.tensor
        def _(t):
            for tt in range(16):
                if tt >= 2:
                    t.wait_ge(self.ve.h, b_ve3 + 3 * (tt - 1))
                self.pe.tick(t.transpose(
                    self.bank(self.PSA, tt % 2)[0:16, 0:128],
                    self.Hres[:, tt, :], self.eye_f[:]))
                self.pe.tick(t.transpose(
                    self.bank(self.PSB, tt % 2)[0:4, 0:128],
                    self.Hpre[:, tt, :], self.eye_f[:]))
                self.pe.tick(t.transpose(
                    self.bank(self.PSB, 2 + tt % 2)[0:4, 0:128],
                    self.Hpost[:, tt, :], self.eye_f[:]))

        @blk.vector
        def _(v):
            for tt in range(16):
                v.wait_ge(self.pe.h, b_pe3 + 3 * tt + 3)
                sl = slice(tt * 128, (tt + 1) * 128)
                v.tensor_copy(HrowsR[0:16, sl],
                              self.bank(self.PSA, tt % 2)[0:16, 0:128])
                v.tensor_copy(HrowsPre[0:4, sl],
                              self.bank(self.PSB, tt % 2)[0:4, 0:128])
                self.ve.tick(v.tensor_copy(
                    HrowsPost[0:4, sl],
                    self.bank(self.PSB, 2 + tt % 2)[0:4, 0:128]), 3)

        self.barrier()
        if _FFN_TAP:
            @blk.gpsimd
            def _(g):
                self.gd.tick(g.dma_start(out=self.DBGA[:, 8, 0:256],
                                         in_=self.Hres[:].rearrange(
                                             "p a b -> p (a b)")), 16)
                self.gd.tick(g.dma_start(out=self.DBGA[:, 9, 0:64],
                                         in_=self.Hpre[:].rearrange(
                                             "p a b -> p (a b)")), 16)
                self.gd.tick(g.dma_start(out=self.DBGA[:, 9, 64:128],
                                         in_=self.Hpost[:].rearrange(
                                             "p a b -> p (a b)")), 16)
                self.gd.tick(g.dma_start(out=self.DBGA[0:16, 10, :],
                                         in_=HrowsR), 16)
                self.gd.tick(g.dma_start(out=self.DBGA[0:4, 11, 0:2048],
                                         in_=HrowsPre), 16)
                self.gd.tick(g.dma_start(out=self.DBGA[0:4, 11, 2048:4096] if False else self.DBGA[4:8, 11, 0:2048],
                                         in_=HrowsPost), 16)
            self.barrier()

        def selT(nrows, r):
            return self.eye_bf[0:nrows, r:r + 1].broadcast_to([nrows, 128])

        for q in range(NQ):
            qs = slice(q * QW, (q + 1) * QW)
            b_pe4 = self.pe.c
            b_se4 = self.se.c

            @blk.tensor
            def _(t, qs=qs):
                for j in range(4):
                    self.pe.tick(t.matmul(
                        self.bank(self.PSA, j), selT(4, j),
                        HrowsPre[0:4, qs], start=True, stop=True))

            @blk.scalar
            def _(s, b_pe4=b_pe4):
                for j in range(4):
                    s.wait_ge(self.pe.h, b_pe4 + j + 1)
                    self.se.tick(s.activation(hv[:, j, :],
                                              self.bank(self.PSA, j), AF.Copy))

            b_se5 = self.se.c
            b_ve5 = self.ve.c

            @blk.vector
            def _(v, q=q, b_se5=b_se5):
                v.wait_ge(self.se.h, b_se5)
                for c in range(8):
                    v.tensor_mul(accf, self.A(0 * 8 + c, q), hv[:, 0, :])
                    for n in range(1, 4):
                        v.tensor_mul(tmpf, self.A(n * 8 + c, q), hv[:, n, :])
                        v.tensor_add(accf, accf, tmpf)
                    self.ve.tick(v.tensor_copy(xpv[:, c, :], accf))

            b_gd5 = self.gd.c
            b_pe5 = self.pe.c

            @blk.gpsimd
            def _(g, b_pe5=b_pe5):
                for cp in range(8):
                    if cp >= 2:
                        g.wait_ge(self.pe.h, b_pe5 + cp - 1)
                    self.gd.tick(g.dma_start(
                        out=wbc[cp % 2].rearrange("p (c n) -> p c n", c=8),
                        in_=self.wbr[:, cp * 128:(cp + 1) * 128].rearrange(
                            "(c p) n -> p c n", p=128)), 16)

            @blk.tensor
            def _(t, b_ve5=b_ve5, b_gd5=b_gd5):
                t.wait_ge(self.ve.h, b_ve5 + 8)
                for cp in range(8):
                    t.wait_ge(self.gd.h, b_gd5 + 16 * (cp + 1))
                    ps = self.bank(self.PSB, cp % 4)
                    wv2 = wbc[cp % 2].rearrange("p (c n) -> p c n", c=8)
                    last = None
                    for c in range(8):
                        last = t.matmul(ps, wv2[:, c, :], xpv[:, c, :],
                                        start=(c == 0), stop=(c == 7))
                    self.pe.tick(last)

            b_se6 = self.se.c

            @blk.scalar
            def _(s, b_pe5=b_pe5):
                for cp in range(8):
                    s.wait_ge(self.pe.h, b_pe5 + cp + 1)
                    self.se.tick(s.activation(yv[:, cp, :],
                                              self.bank(self.PSB, cp % 4),
                                              AF.Copy))

            if _FFN_TAP and q == 0:
                @blk.gpsimd
                def _(g, b_se6=b_se6):
                    g.wait_ge(self.se.h, b_se6 + 8)
                    self.gd.tick(g.dma_start(out=self.DBGA[:, 13, :],
                                             in_=xpre[:, 0:2048]), 16)
                    self.gd.tick(g.dma_start(out=self.DBGA[:, 14, :],
                                             in_=xpre[:, 2048:4096]), 16)
                    self.gd.tick(g.dma_start(out=self.DBGA[:, 15, :],
                                             in_=ybf[:, 0:2048]), 16)
                    self.gd.tick(g.dma_start(out=self.DBGA[:, 16, :],
                                             in_=ybf[:, 2048:4096]), 16)
                self.barrier()

            for n in range(4):
                b_pe6 = self.pe.c
                b_se7 = self.se.c

                @blk.tensor
                def _(t, n=n, qs=qs, b_se7=b_se7):
                    t.wait_ge(self.se.h, b_se7)
                    for j in range(4):
                        self.pe.tick(t.matmul(
                            self.bank(self.PSA, j), selT(16, n * 4 + j),
                            HrowsR[0:16, qs], start=True, stop=True))
                    self.pe.tick(t.matmul(
                        self.bank(self.PSB, 0), selT(4, n),
                        HrowsPost[0:4, qs], start=True, stop=True))

                b_ve7 = self.ve.c

                @blk.scalar
                def _(s, n=n, b_pe6=b_pe6, b_ve7=b_ve7):
                    if n > 0:
                        s.wait_ge(self.ve.h, b_ve7)
                    for j in range(4):
                        s.wait_ge(self.pe.h, b_pe6 + j + 1)
                        self.se.tick(s.activation(hv[:, j, :],
                                                  self.bank(self.PSA, j),
                                                  AF.Copy))
                    s.wait_ge(self.pe.h, b_pe6 + 5)
                    self.se.tick(s.activation(hv[:, 4, :],
                                              self.bank(self.PSB, 0), AF.Copy))

                b_se8 = self.se.c
                b_sd8 = self.sd.c
                b_ve8 = self.ve.c

                @blk.sync
                def _(sy, n=n, q=q, qs=qs, b_ve8=b_ve8):
                    for c in range(8):
                        if c >= 1:
                            sy.wait_ge(self.ve.h, b_ve8 + c)
                        self.sd.tick(sy.dma_start(
                            out=xof, in_=xold[:, n * 8 + c, qs]), 16)
                        sy.wait_ge(self.ve.h, b_ve8 + c + 1)
                        dst = (self.OUT.rearrange("(i p) t -> p i t", p=128)
                               if fin else xnew)
                        self.sd.tick(sy.dma_start(
                            out=dst[:, n * 8 + c, qs], in_=accf), 16)

                @blk.vector
                def _(v, n=n, q=q, b_se8=b_se8, b_sd8=b_sd8):
                    for c in range(8):
                        v.wait_ge(self.se.h, b_se8)
                        v.wait_ge(self.sd.h, b_sd8 + 16 * (2 * c + 1))
                        v.tensor_mul(accf, self.A(0 * 8 + c, q), hv[:, 0, :])
                        for j in range(1, 4):
                            v.tensor_mul(tmpf, self.A(j * 8 + c, q),
                                         hv[:, j, :])
                            v.tensor_add(accf, accf, tmpf)
                        v.tensor_mul(tmpf, yv[:, c, :], hv[:, 4, :])
                        v.tensor_add(accf, accf, tmpf)
                        self.ve.tick(v.tensor_add(accf, accf, xof))

            self.barrier()

    def ph_final(self):
        blk = self.blk
        sd_c = self.sd.c
        gd_c = self.gd.c

        @blk.sync
        def _(sy):
            sy.wait_ge(self.sd.h, sd_c)

        @blk.gpsimd
        def _(g):
            g.wait_ge(self.gd.h, gd_c)


def build_program(upto=99, dump_a_after=None):
    b = B()

    def dump_a():
        blk = b.blk

        @blk.sync
        def _(sy):
            for kk in range(4):
                b.sd.tick(sy.dma_start(
                    out=b.DBGA[:, kk * 8:(kk + 1) * 8, :],
                    in_=b.A()[:, kk * 8:(kk + 1) * 8, :]), 16)
        b.barrier()

    steps = []
    steps.append(lambda: b.ph_init())
    steps.append(lambda: b.ph_stats(None, need_rho=False))
    steps.append(lambda: b.ph_apply_ln(None))
    steps.append(lambda: b.ph_proj_fm(b.waq, lambda d, e, sy, ctx: b.sd.tick(
        sy.dma_start(out=b.SPQ[:, d, :], in_=e), 16),
        bias_col=lambda d: b.ba_s[:, d:d + 1]))
    steps.append(lambda: b.ph_proj_fm(b.wak, lambda d, e, sy, ctx: b.sd.tick(
        sy.dma_start(out=b.SPK[:, d, :], in_=e), 16),
        bias_col=lambda d: b.ba_s[:, 32 + d:32 + d + 1]))
    steps.append(lambda: b.ph_proj_fm(b.wav, lambda d, e, sy, ctx: b.sd.tick(
        sy.dma_start(out=b.SPV[:, d, :], in_=e), 16),
        bias_col=lambda d: b.ba_s[:, 64 + d:64 + d + 1]))
    steps.append(lambda: b.ph_attn_a())
    steps.append(lambda: b.ph_proj_resid(
        b.wao, b.xT.rearrange("(i p) t -> p i t", p=128), b.SPX1,
        bias_col=lambda d: b.ba_s[:, 96 + d:96 + d + 1]))
    steps.append(lambda: b.ph_stats(b.SPX1, need_rho=True))
    steps.append(lambda: b.ph_apply_ln(b.SPX1))
    steps.append(lambda: b.ph_ffn(b.SPX1, b.SPX2))
    steps.append(lambda: b.ph_stats(b.SPX2, need_rho=False))
    steps.append(lambda: b.ph_apply_ln(b.SPX2))
    steps.append(lambda: b.ph_proj_fm(b.wbk, lambda d, e, sy, ctx: b.sd.tick(
        sy.dma_start(out=b.KTL[d][:, 0:T], in_=e), 16)))
    steps.append(lambda: b.ph_proj_v_shared())
    steps.append(lambda: b.ph_proj_fm(b.wbq, lambda d, e, sy, ctx: b.sd.tick(
        sy.dma_start(out=b.SPQ2[:, d, :], in_=e), 16)))
    steps.append(lambda: b.ph_barrier_kv())
    steps.append(lambda: b.ph_reload_A(b.SPQ2))
    steps.append(lambda: b.ph_attn_b())
    steps.append(lambda: b.ph_reload_A(b.SPAV2))
    steps.append(lambda: b.ph_proj_resid(b.wbo, b.SPX2, b.SPX3))
    steps.append(lambda: b.ph_stats(b.SPX3, need_rho=True))
    steps.append(lambda: b.ph_apply_ln(b.SPX3))
    steps.append(lambda: b.ph_ffn(b.SPX3, None, fin=True))
    for i, s in enumerate(steps[:upto]):
        s()
        if dump_a_after is not None and i + 1 == dump_a_after:
            dump_a()
    b.ph_final()
    return b


def _host_inputs(inputs):
    x = np.asarray(inputs["L0_attn_a_x"])
    f = np.float32
    wT = {
        "waq": np.ascontiguousarray(np.asarray(inputs["L0_attn_a_Wq"]).T),
        "wak": np.ascontiguousarray(np.asarray(inputs["L0_attn_a_Wk"]).T),
        "wav": np.ascontiguousarray(np.asarray(inputs["L0_attn_a_Wv"]).T),
        "wao": np.ascontiguousarray(np.asarray(inputs["L0_attn_a_Wo"]).T),
        "wbq": np.ascontiguousarray(np.asarray(inputs["L0_attn_b_w_q"]).T),
        "wbk": np.ascontiguousarray(np.asarray(inputs["L0_attn_b_w_k"]).T),
        "wbv": np.ascontiguousarray(np.asarray(inputs["L0_attn_b_w_v"]).T),
        "wbo": np.ascontiguousarray(np.asarray(inputs["L0_attn_b_w_o"]).T),
    }
    wcat = np.ascontiguousarray(np.concatenate(
        [np.asarray(inputs["L0_ffn_W_res"]), np.asarray(inputs["L0_ffn_W_pre"]),
         np.asarray(inputs["L0_ffn_W_post"])], axis=0).T)
    wbr = np.ascontiguousarray(np.asarray(inputs["L0_ffn_W_branch"]))
    blay = {
        "b_aq": np.ascontiguousarray(
            np.asarray(inputs["L0_attn_a_bq"]).reshape(32, 128).T),
        "b_ak": np.ascontiguousarray(
            np.asarray(inputs["L0_attn_a_bk"]).reshape(32, 128).T),
        "b_av": np.ascontiguousarray(
            np.asarray(inputs["L0_attn_a_bv"]).reshape(32, 128).T),
        "b_ao": np.ascontiguousarray(
            np.asarray(inputs["L0_attn_a_bo"]).reshape(32, 128).T),
    }
    hbase = np.ascontiguousarray(np.concatenate(
        [np.asarray(inputs["L0_ffn_H_res_base"]).reshape(-1),
         np.asarray(inputs["L0_ffn_H_pre_base"]).reshape(-1),
         np.asarray(inputs["L0_ffn_H_post_base"]).reshape(-1)]).reshape(24, 1))
    gate = np.asarray(inputs["L0_ffn_gate"]).reshape(1, 1).astype(f)
    eye = np.eye(128, dtype=f)
    eye4 = np.ascontiguousarray(np.eye(4, dtype=f).reshape(1, 16))
    nonce = np.full((1, 8), int(np.random.randint(1, 2 ** 30)), np.int32)

    in_maps = []
    for c in range(N_CORE):
        bb, hf = c // 2, c % 2
        m = {"xT": np.ascontiguousarray(
            np.asarray(x[bb, hf * T:(hf + 1) * T, :]).T.astype(f))}
        m.update(wT)
        m.update(blay)
        m.update(dict(wcat=wcat, wbr=wbr, hbase=hbase, gate=gate, eye=eye,
                      eye4=eye4, nonce=nonce,
                      cid=np.full((1, 8), c, np.int32)))
        in_maps.append(m)
    return in_maps


def kernel(**inputs):
    global _LAST
    if "nc" not in _CACHE:
        _CACHE["nc"] = build_program()
    b = _CACHE["nc"]
    in_maps = _host_inputs(inputs)
    res = run_bass_kernel_spmd(b.nc, in_maps, core_ids=list(range(N_CORE)))
    _LAST = res
    out = np.empty((4, D, D), np.float32)
    for c in range(N_CORE):
        bb, hf = c // 2, c % 2
        out[bb, hf * T:(hf + 1) * T, :] = np.asarray(res.results[c]["OUT"]).T
    return out


# version 02
# speedup vs baseline: 1.0269x; 1.0269x over previous
import contextlib
import numpy as np
import concourse.bass as bass
import concourse.mybir as mybir
from concourse.bass import ds
from concourse.bass_utils import run_bass_kernel_spmd

F32 = mybir.dt.float32
BF16 = mybir.dt.bfloat16
I32 = mybir.dt.int32
AF = mybir.ActivationFunctionType
ALU = mybir.AluOpType
AX = mybir.AxisListType
ET = mybir.EngineType

D = 4096
T = 2048
NCH = 32
NQ = 4
QW = 512
H = 32
HD = 128
INV_S128 = float(1.0 / np.sqrt(128.0))
THR = float(0.001 * np.sqrt(128.0))
CB = 6.0
EPS_LN = 1e-5
EPS_RMS = 1.1920929e-07
SINK_ITERS = 50
INV_TAU = 10.0
N_CORE = 8

_DEBUG = False
_ATTNA_TAP = False
_FFN_TAP = False
_ATTNA_SUB = 99
_LAST = {}
_CACHE = {}


class Sem:
    def __init__(self, nc, es, name):
        self.h = nc.semaphore(name)
        self.h = es.enter_context(self.h)
        self.c = 0

    def tick(self, inst, k=1):
        inst.then_inc(self.h, k)
        self.c += k
        return self.c


class B:
    def __init__(self):
        nc = bass.Bass("TRN2", target_bir_lowering=False)
        self.nc = nc
        es = contextlib.ExitStack()
        self.es = es

        def din(name, shape, dt=F32):
            return nc.dram_tensor(name, shape, dt, kind="ExternalInput").ap()

        self.xT = din("xT", [D, T])
        self.waq = din("waq", [D, D]); self.wak = din("wak", [D, D])
        self.wav = din("wav", [D, D]); self.wao = din("wao", [D, D])
        self.wbq = din("wbq", [D, D]); self.wbk = din("wbk", [D, D])
        self.wbv = din("wbv", [D, D]); self.wbo = din("wbo", [D, D])
        self.wcat = din("wcat", [D, 24])
        self.wbr = din("wbr", [1024, 1024])
        self.b_aq = din("b_aq", [128, 32]); self.b_ak = din("b_ak", [128, 32])
        self.b_av = din("b_av", [128, 32]); self.b_ao = din("b_ao", [128, 32])
        self.hbase = din("hbase", [24, 1])
        self.gate = din("gate", [1, 1])
        self.eye = din("eye", [128, 128])
        self.eye4 = din("eye4", [1, 16])
        self.nonce = din("nonce", [1, 8], I32)
        self.cid = din("cid", [1, 8], I32)

        self.OUT = nc.dram_tensor("OUT", [D, T], F32, kind="ExternalOutput").ap()

        kind = "ExternalOutput" if _DEBUG else "Internal"

        def dloc(name, shape, dt):
            return nc.dram_tensor(name, shape, dt, kind=kind).ap()

        self.SPQ = dloc("SPQ", [128, NCH, T], BF16)
        self.SPK = dloc("SPK", [128, NCH, T], BF16)
        self.SPV = dloc("SPV", [128, NCH, T], BF16)
        self.SPQ2 = dloc("SPQ2", [128, NCH, T], BF16)
        self.SPAV2 = dloc("SPAV2", [128, NCH, T], BF16)
        self.SPX1 = dloc("SPX1", [128, NCH, T], F32)
        self.SPX2 = dloc("SPX2", [128, NCH, T], F32)
        self.SPX3 = dloc("SPX3", [128, NCH, T], F32)
        self.DBGA = dloc("DBGA", [128, NCH, T], BF16)

        self.KT8 = nc.dram_tensor("KT8", [N_CORE, H, 128, T], BF16,
                                  kind="Internal", addr_space="Shared").ap()
        self.V8 = nc.dram_tensor("V8", [N_CORE, H, 16, 128, 128], BF16,
                                 kind="Internal", addr_space="Shared").ap()
        self.KTL = nc.dram_tensor("KTL", [H, 128, 2 * T], BF16,
                                  kind="Internal").ap()
        self.VL = nc.dram_tensor("VL", [H, 32, 128, 128], BF16,
                                 kind="Internal").ap()
        self.FLG = nc.dram_tensor("FLG", [N_CORE, 1, 8], I32,
                                  kind="Internal", addr_space="Shared").ap()

        AR_COLS = 97280
        self.AR = es.enter_context(nc.sbuf_tensor("AR", [128, AR_COLS], BF16))
        self.S0 = 65536

        self.ones_sq = es.enter_context(nc.sbuf_tensor("ones_sq", [128, 128], BF16))

        self.eye_bf = es.enter_context(nc.sbuf_tensor("eye_bf", [128, 128], BF16))
        self.eye_f = es.enter_context(nc.sbuf_tensor("eye_f", [128, 128], F32))
        self.ones_bf = es.enter_context(nc.sbuf_tensor("ones_bf", [128, 1], BF16))
        self.ones_f = es.enter_context(nc.sbuf_tensor("ones_f", [1, 128], F32))
        self.ba_s = es.enter_context(nc.sbuf_tensor("ba_s", [128, 128], F32))
        self.hb_s = es.enter_context(nc.sbuf_tensor("hb_s", [24, 1], F32))
        self.wcat_s = es.enter_context(nc.sbuf_tensor("wcat_s", [128, 32, 24], BF16))
        self.cst = es.enter_context(nc.sbuf_tensor("cst", [128, 4], F32))
        self.g_b = es.enter_context(nc.sbuf_tensor("g_b", [128, 2], F32))
        self.eye4b = es.enter_context(nc.sbuf_tensor("eye4b", [128, 16], F32))
        self.gt_s = es.enter_context(nc.sbuf_tensor("gt_s", [1, 2], F32))
        self.e4_s = es.enter_context(nc.sbuf_tensor("e4_s", [1, 16], F32))
        self.nz_s = es.enter_context(nc.sbuf_tensor("nz_s", [1, 8], I32))
        self.cid_s = es.enter_context(nc.sbuf_tensor("cid_s", [1, 8], I32))
        self.fr_s = es.enter_context(nc.sbuf_tensor("fr_s", [1, 8], I32))
        self.Zres = es.enter_context(nc.sbuf_tensor("Zres", [128, 16, 16], F32))
        self.Zpre = es.enter_context(nc.sbuf_tensor("Zpre", [128, 16, 4], F32))
        self.Zpost = es.enter_context(nc.sbuf_tensor("Zpost", [128, 16, 4], F32))
        self.Ebuf = es.enter_context(nc.sbuf_tensor("Ebuf", [128, 16, 16], F32))
        self.lnSu = es.enter_context(nc.sbuf_tensor("lnSu", [128, 16, 4], F32))
        self.lnSv = es.enter_context(nc.sbuf_tensor("lnSv", [128, 16, 4], F32))
        self.Sred = es.enter_context(nc.sbuf_tensor("Sred", [128, 16, 4], F32))
        self.Mred = es.enter_context(nc.sbuf_tensor("Mred", [128, 16, 4], F32))
        self.Hpre = es.enter_context(nc.sbuf_tensor("Hpre", [128, 16, 4], F32))
        self.Hpost = es.enter_context(nc.sbuf_tensor("Hpost", [128, 16, 4], F32))
        self.Hres = es.enter_context(nc.sbuf_tensor("Hres", [128, 16, 16], F32))
        self.sm4 = es.enter_context(nc.sbuf_tensor("sm4", [128, 16, 1], F32))

        self.PSA = es.enter_context(nc.psum_tensor("PSA", [128, 2048], F32))
        self.PSB = es.enter_context(nc.psum_tensor("PSB", [128, 2048], F32))

        self.pe = Sem(nc, es, "pe")
        self.ve = Sem(nc, es, "ve")
        self.se = Sem(nc, es, "se")
        self.gd = Sem(nc, es, "gd")
        self.sd = Sem(nc, es, "sd")
        self.sp = Sem(nc, es, "sp")
        self.bar = {e: Sem(nc, es, f"bar_{e}") for e in
                    ["tensor", "vector", "scalar", "gpsimd", "sync"]}
        self.nbar = 0

        self.blk = es.enter_context(nc.Block())

    def A(self, i=None, q=None):
        v = self.AR[:, 0:65536].rearrange("p (i t) -> p i t", i=NCH)
        if i is None:
            return v
        if q is None:
            return v[:, i, :]
        return v[:, i, q * QW:(q + 1) * QW]

    def S(self, off, cols, dtype=BF16, parts=128):
        v = self.AR[0:parts, self.S0 + off:self.S0 + off + cols]
        if dtype == F32:
            assert off % 2 == 0 and cols % 2 == 0
            v = v.bitcast(F32)
        return v

    def bank(self, ps, j):
        return ps[:, j * 512:(j + 1) * 512]

    def pid_of(self, eng):
        key = id(eng)
        if not hasattr(self, "_pidcache"):
            self._pidcache = {}
        if key not in self._pidcache:
            v = eng.value_load(self.cid_s[0:1, 0:1])
            self._pidcache[key] = self.nc.s_assert_within(
                v, 0, 7, skip_runtime_assert=True)
        return self._pidcache[key]

    def barrier(self):
        self.nbar += 1
        snap = [(self.pe, self.pe.c), (self.ve, self.ve.c), (self.se, self.se.c),
                (self.gd, self.gd.c), (self.sd, self.sd.c)]
        blk = self.blk
        for ename in ["tensor", "vector", "scalar", "gpsimd", "sync"]:
            def mk():
                def f(e):
                    for sem, val in snap:
                        if val > 0:
                            e.wait_ge(sem.h, val)
                return f
            getattr(blk, ename)(mk())

    def ph_init(self):
        blk = self.blk

        @blk.gpsimd
        def _(g):
            self.gd.tick(g.dma_start(out=self.eye_bf[:], in_=self.eye), 16)
            self.gd.tick(g.dma_start(
                out=self.wcat_s[:],
                in_=self.wcat.rearrange("(i p) n -> p i n", p=128)), 16)
            xsrc = self.xT.rearrange("(i p) t -> p i t", p=128)
            for kk in range(4):
                self.gd.tick(g.dma_start(
                    out=self.A()[:, kk * 8:(kk + 1) * 8, :],
                    in_=xsrc[:, kk * 8:(kk + 1) * 8, :]), 16)

        @blk.sync
        def _(sy):
            self.sd.tick(sy.dma_start(out=self.eye_f[:], in_=self.eye), 16)
            self.sd.tick(sy.dma_start(out=self.ba_s[:, 0:32], in_=self.b_aq), 16)
            self.sd.tick(sy.dma_start(out=self.ba_s[:, 32:64], in_=self.b_ak), 16)
            self.sd.tick(sy.dma_start(out=self.ba_s[:, 64:96], in_=self.b_av), 16)
            self.sd.tick(sy.dma_start(out=self.ba_s[:, 96:128], in_=self.b_ao), 16)
            self.sd.tick(sy.dma_start(out=self.hb_s[:], in_=self.hbase), 16)
            self.sd.tick(sy.dma_start(out=self.gt_s[0:1, 0:1], in_=self.gate), 16)
            self.sd.tick(sy.dma_start(out=self.nz_s[:], in_=self.nonce), 16)
            self.sd.tick(sy.dma_start(out=self.cid_s[:], in_=self.cid), 16)
            self.sd.tick(sy.dma_start(out=self.e4_s[:], in_=self.eye4), 16)

        @blk.vector
        def _(v):
            v.memset(self.ones_bf[:], 1.0)
            v.memset(self.ones_f[:], 1.0)
            v.memset(self.cst[:, 0:1], INV_S128)
            v.memset(self.cst[:, 1:2], -CB)
            v.memset(self.cst[:, 2:3], -1.0)
            v.memset(self.cst[:, 3:4], -0.5)
            self.ve.tick(v.memset(self.ones_sq[:], 1.0))

        self.barrier()
        b_se = self.se.c

        @blk.scalar
        def _(s):
            self.se.tick(s.activation(self.gt_s[0:1, 1:2], self.gt_s[0:1, 0:1],
                                      AF.Sigmoid))

        b_pe = self.pe.c

        @blk.tensor
        def _(t):
            t.wait_ge(self.se.h, b_se + 1)
            t.matmul(self.bank(self.PSA, 0)[:, 0:1], self.ones_f[:],
                     self.gt_s[0:1, 1:2], start=True, stop=True)
            self.pe.tick(t.matmul(self.bank(self.PSA, 1)[:, 0:16], self.ones_f[:],
                                  self.e4_s[:], start=True, stop=True))

        @blk.vector
        def _(v):
            v.wait_ge(self.pe.h, b_pe + 1)
            v.tensor_copy(self.g_b[:, 0:1], self.bank(self.PSA, 0)[:, 0:1])
            v.tensor_copy(self.eye4b[:], self.bank(self.PSA, 1)[:, 0:16])
            self.ve.tick(v.tensor_scalar(self.g_b[:, 1:2], self.g_b[:, 0:1],
                                         -1.0, 1.0, ALU.mult, ALU.add))

        self.barrier()

    def ph_stats(self, src, need_rho):
        blk = self.blk
        sq0 = self.S(8192, 2048)
        sq1 = self.S(10240, 2048)
        cs0 = self.S(12288, 2048)
        cs1 = self.S(14336, 2048)
        b_pe, b_se, b_gd = self.pe.c, self.se.c, self.gd.c

        if src is not None:
            @blk.gpsimd
            def _(g):
                for i in range(NCH):
                    if i >= 2:
                        g.wait_ge(self.pe.h, b_pe + (i - 1))
                    self.gd.tick(g.dma_start(
                        out=(cs0 if i % 2 == 0 else cs1), in_=src[:, i, :]), 16)

        @blk.scalar
        def _(s):
            for i in range(NCH):
                if src is not None:
                    s.wait_ge(self.gd.h, b_gd + 16 * (i + 1))
                    xin = cs0 if i % 2 == 0 else cs1
                else:
                    xin = self.A(i)
                if i >= 2:
                    s.wait_ge(self.pe.h, b_pe + (i - 1))
                self.se.tick(s.activation(sq0 if i % 2 == 0 else sq1, xin,
                                          AF.Square))

        @blk.tensor
        def _(t):
            for i in range(NCH):
                t.wait_ge(self.se.h, b_se + (i + 1))
                if src is not None:
                    xin = cs0 if i % 2 == 0 else cs1
                else:
                    xin = self.A(i)
                sq = sq0 if i % 2 == 0 else sq1
                last = None
                for q in range(NQ):
                    t.matmul(self.bank(self.PSA, q), self.ones_sq[:],
                             xin[:, q * QW:(q + 1) * QW],
                             start=(i == 0), stop=(i == NCH - 1))
                    last = t.matmul(self.bank(self.PSB, q),
                                    self.ones_sq[:], sq[:, q * QW:(q + 1) * QW],
                                    start=(i == 0), stop=(i == NCH - 1))
                self.pe.tick(last)

        self.barrier()
        mb = self.S(0, 4096, F32)
        sb = self.S(4096, 4096, F32)
        rho_b = self.S(16384, 4096, F32)
        dstf = self.S(8192, 4096, F32)
        vsc = self.S(12288, 4096, F32)
        b_ve1 = self.ve.c
        b_se1 = self.se.c

        @blk.vector
        def _(v):
            v.tensor_scalar_mul(mb, self.PSA[:, :], 1.0 / D)
            v.tensor_scalar_mul(dstf, self.PSB[:, :], 1.0 / D)
            v.tensor_mul(vsc, mb, mb)
            v.tensor_sub(dstf, dstf, vsc)
            self.ve.tick(v.tensor_scalar_add(vsc, dstf, EPS_LN))

        @blk.scalar
        def _(s):
            s.wait_ge(self.ve.h, b_ve1 + 1)
            s.activation(vsc, vsc, AF.Ln)
            self.se.tick(s.activation(sb, vsc, AF.Exp,
                                      scale=self.cst[:, 3:4]))
            if need_rho:
                self.se.tick(s.activation(vsc, vsc, AF.Exp,
                                          scale=self.cst[:, 2:3]))

        if need_rho:
            b_se2 = self.se.c
            b_ve2 = self.ve.c

            @blk.vector
            def _(v):
                v.wait_ge(self.se.h, b_se2)
                v.tensor_mul(dstf, dstf, vsc)
                self.ve.tick(v.tensor_scalar_add(dstf, dstf, EPS_RMS))

            @blk.scalar
            def _(s):
                s.wait_ge(self.ve.h, b_ve2 + 1)
                s.activation(dstf, dstf, AF.Ln)
                self.se.tick(s.activation(rho_b, dstf, AF.Exp,
                                          scale=self.cst[:, 3:4]))

        self.barrier()

    def ph_apply_ln(self, src):
        blk = self.blk
        mb = self.S(0, 4096, F32)
        sb = self.S(4096, 4096, F32)
        dst_f = self.S(8192, 4096, F32)
        cs0 = self.S(12288, 2048)
        cs1 = self.S(14336, 2048)
        b_gd, b_ve = self.gd.c, self.ve.c

        if src is not None:
            @blk.gpsimd
            def _(g):
                for i in range(NCH):
                    if i >= 2:
                        g.wait_ge(self.ve.h, b_ve + 2 * (i - 1))
                    self.gd.tick(g.dma_start(
                        out=(cs0 if i % 2 == 0 else cs1), in_=src[:, i, :]), 16)

        @blk.vector
        def _(v):
            for i in range(NCH):
                if src is None:
                    xin = self.A(i)
                else:
                    v.wait_ge(self.gd.h, b_gd + 16 * (i + 1))
                    xin = cs0 if i % 2 == 0 else cs1
                v.tensor_sub(dst_f, xin, mb)
                self.ve.tick(v.tensor_mul(self.A(i), dst_f, sb), 2)

        self.barrier()

    def ph_proj_fm(self, wt, out_cb, bias_col=None):
        blk = self.blk
        wb = [self.S(0, 8192), self.S(8192, 8192)]
        est = [self.S(16384, 2048), self.S(18432, 2048)]
        b_gd, b_pe, b_se, b_sd = self.gd.c, self.pe.c, self.se.c, self.sd.c

        @blk.gpsimd
        def _(g):
            for oc in range(16):
                if oc >= 2:
                    g.wait_ge(self.se.h, b_se + 8 * (oc - 1))
                self.gd.tick(g.dma_start(
                    out=wb[oc % 2].rearrange("p (i n) -> p i n", i=NCH),
                    in_=wt[:, oc * 256:(oc + 1) * 256].rearrange(
                        "(i p) n -> p i n", p=128)), 16)

        @blk.tensor
        def _(t):
            for oc in range(16):
                t.wait_ge(self.gd.h, b_gd + 16 * (oc + 1))
                if oc >= 1:
                    t.wait_ge(self.se.h, b_se + 8 * (oc - 1) + 8)
                wv = wb[oc % 2].rearrange("p (i n) -> p i n", i=NCH)
                for sub in range(2):
                    ps = self.PSA if sub == 0 else self.PSB
                    last = None
                    for i in range(NCH):
                        for q in range(NQ):
                            last = t.matmul(
                                self.bank(ps, q),
                                wv[:, i, sub * 128:(sub + 1) * 128],
                                self.A(i, q),
                                start=(i == 0), stop=(i == NCH - 1))
                    self.pe.tick(last)

        @blk.scalar
        def _(s):
            for oc in range(16):
                for sub in range(2):
                    ps = self.PSA if sub == 0 else self.PSB
                    d128 = oc * 2 + sub
                    if oc >= 1:
                        s.wait_ge(self.sd.h, b_sd + 16 * (2 * (oc - 1) + sub + 1))
                    for q in range(NQ):
                        s.wait_ge(self.pe.h, b_pe + oc * 2 + sub + 1)
                        if bias_col is not None:
                            self.se.tick(s.activation(
                                est[sub][:, q * QW:(q + 1) * QW],
                                self.bank(ps, q), AF.Identity,
                                bias=bias_col(d128)))
                        else:
                            self.se.tick(s.activation(
                                est[sub][:, q * QW:(q + 1) * QW],
                                self.bank(ps, q), AF.Copy))

        @blk.sync
        def _(sy):
            ctx = {}
            for oc in range(16):
                for sub in range(2):
                    d128 = oc * 2 + sub
                    sy.wait_ge(self.se.h, b_se + d128 * 4 + 4)
                    out_cb(d128, est[sub], sy, ctx)

        self.barrier()

    def ph_proj_resid(self, wt, xold, xnew, bias_col=None):
        blk = self.blk
        wb = [self.S(0, 8192), self.S(8192, 8192)]
        ev = [self.S(16384 + 1024 * k, 1024, F32) for k in range(4)]
        xo = [self.S(20480 + 1024 * k, 1024, F32) for k in range(4)]
        b_gd, b_pe, b_se, b_sd, b_ve = (self.gd.c, self.pe.c, self.se.c,
                                        self.sd.c, self.ve.c)

        @blk.gpsimd
        def _(g):
            for oc in range(16):
                if oc >= 2:
                    g.wait_ge(self.se.h, b_se + 8 * (oc - 1))
                self.gd.tick(g.dma_start(
                    out=wb[oc % 2].rearrange("p (i n) -> p i n", i=NCH),
                    in_=wt[:, oc * 256:(oc + 1) * 256].rearrange(
                        "(i p) n -> p i n", p=128)), 16)

        @blk.tensor
        def _(t):
            for oc in range(16):
                t.wait_ge(self.gd.h, b_gd + 16 * (oc + 1))
                if oc >= 1:
                    t.wait_ge(self.se.h, b_se + 8 * (oc - 1) + 8)
                wv = wb[oc % 2].rearrange("p (i n) -> p i n", i=NCH)
                for sub in range(2):
                    ps = self.PSA if sub == 0 else self.PSB
                    last = None
                    for i in range(NCH):
                        for q in range(NQ):
                            last = t.matmul(
                                self.bank(ps, q),
                                wv[:, i, sub * 128:(sub + 1) * 128],
                                self.A(i, q),
                                start=(i == 0), stop=(i == NCH - 1))
                    self.pe.tick(last)

        @blk.scalar
        def _(s):
            for d128 in range(32):
                oc, sub = d128 // 2, d128 % 2
                ps = self.PSA if sub == 0 else self.PSB
                for q in range(NQ):
                    k = d128 * 4 + q
                    if k >= 4:
                        s.wait_ge(self.sd.h, b_sd + 16 * (2 * (k - 4) + 2))
                    s.wait_ge(self.pe.h, b_pe + oc * 2 + sub + 1)
                    if bias_col is not None:
                        self.se.tick(s.activation(ev[k % 4], self.bank(ps, q),
                                                  AF.Identity,
                                                  bias=bias_col(d128)))
                    else:
                        self.se.tick(s.activation(ev[k % 4], self.bank(ps, q),
                                                  AF.Copy))

        @blk.vector
        def _(v):
            for d128 in range(32):
                for q in range(NQ):
                    k = d128 * 4 + q
                    v.wait_ge(self.se.h, b_se + k + 1)
                    v.wait_ge(self.sd.h, b_sd + 16 * (2 * k + 1))
                    self.ve.tick(v.tensor_add(ev[k % 4], ev[k % 4], xo[k % 4]))

        @blk.sync
        def _(sy):
            for d128 in range(32):
                for q in range(NQ):
                    k = d128 * 4 + q
                    if k >= 4:
                        sy.wait_ge(self.ve.h, b_ve + (k - 4) + 1)
                    self.sd.tick(sy.dma_start(
                        out=xo[k % 4], in_=xold[:, d128, q * QW:(q + 1) * QW]), 16)
                    sy.wait_ge(self.ve.h, b_ve + k + 1)
                    self.sd.tick(sy.dma_start(
                        out=xnew[:, d128, q * QW:(q + 1) * QW], in_=ev[k % 4]), 16)

        self.barrier()

    def ph_attn_a(self):
        blk = self.blk
        TQ = 256
        qc = self.S(0, 8192)
        kc = self.S(8192, 8192)
        vc = self.S(16384, 8192)
        pm = self.S(24576, 512)
        pex = self.S(25088, 512)
        pbf = self.S(25600, 512)
        pT = self.S(26112, 512)
        sums = self.S(26624, 32, F32)
        rec = self.S(26688, 32, F32)
        rmx = self.S(26720, 32, F32)
        vt = self.S(26752, 2048)
        df = self.S(28800, 1024, F32)
        dfv = df.rearrange("p (a b) -> p a b", a=16)

        qv = qc.rearrange("p (i t) -> p i t", i=NCH)
        kv_ = kc.rearrange("p (i t) -> p i t", i=NCH)
        vv = vc.rearrange("p (i t) -> p i t", i=NCH)
        vtv = vt.rearrange("p (g e) -> p g e", g=16)

        for tc in range(8):
            b_gd = self.gd.c

            @blk.gpsimd
            def _(g, tc=tc):
                self.gd.tick(g.dma_start(
                    out=qc, in_=self.SPQ[:, :, tc * TQ:(tc + 1) * TQ]), 16)
                self.gd.tick(g.dma_start(
                    out=kc, in_=self.SPK[:, :, tc * TQ:(tc + 1) * TQ]), 16)
                self.gd.tick(g.dma_start(
                    out=vc, in_=self.SPV[:, :, tc * TQ:(tc + 1) * TQ]), 16)

            for g64 in range(4):
                sb_ = self.bank(self.PSA, g64 % 2)
                b_pe0 = self.pe.c
                b_se0 = self.se.c
                b_ve0 = self.ve.c

                def vtb_of(self, grp):
                    bnk = self.bank(self.PSA, 2 + grp % 2).bitcast(BF16)
                    off = ((grp // 2) % 4) * 128
                    return bnk[:, off:off + 128]

                @blk.tensor
                def _(t, tc=tc, g64=g64, sb_=sb_, b_gd=b_gd,
                      b_pe0=b_pe0, b_ve0=b_ve0):
                    t.wait_ge(self.gd.h, b_gd + 48)
                    last = None
                    for grp in range(16):
                        for j in range(4):
                            tok = g64 * 64 + grp * 4 + j
                            last = t.matmul(
                                sb_[j * 32:(j + 1) * 32,
                                    grp * 32:(grp + 1) * 32],
                                qv[:, :, tok], kv_[:, :, tok],
                                start=True, stop=True,
                                tile_position=(0, j * 32))
                    self.pe.tick(last)
                    for grp in range(16):
                        if grp >= 2:
                            t.wait_ge(self.ve.h, b_ve0 + 1 + (grp - 2) + 1)
                        last = None
                        for j in range(4):
                            tok = g64 * 64 + grp * 4 + j
                            last = t.transpose(
                                vtb_of(self, grp)[j * 32:(j + 1) * 32, :],
                                vv[:, :, tok], self.eye_bf[:],
                                tile_position=(0, j * 32))
                        self.pe.tick(last)

                @blk.vector
                def _(v, sb_=sb_, b_pe0=b_pe0, b_se0=b_se0):
                    sbv = sb_.rearrange("p (a b) -> p a b", a=16)
                    v.wait_ge(self.pe.h, b_pe0 + 1)
                    v.tensor_single_scalar(pm, sb_, THR, ALU.is_gt)
                    v.tensor_reduce(
                        rmx.rearrange("p (a b) -> p a b", a=16)[:, :, 0:1],
                        sbv, AX.X, ALU.max)
                    self.ve.tick(v.tensor_sub(
                        dfv, sbv,
                        rmx.rearrange("p (a b) -> p a b", a=16)[
                            :, :, 0:1].broadcast_to([128, 16, 32])))
                    for grp in range(16):
                        v.wait_ge(self.pe.h, b_pe0 + 2 + grp)
                        self.ve.tick(v.tensor_copy(vtv[:, grp, :],
                                                   vtb_of(self, grp)))
                    v.wait_ge(self.se.h, b_se0 + 1)
                    v.tensor_mul(pbf, pex, pm)
                    self.ve.tick(v.tensor_reduce(
                        sums.rearrange("p (a b) -> p a b", a=16)[:, :, 0:1],
                        pbf.rearrange("p (a b) -> p a b", a=16), AX.X, ALU.add))
                    v.wait_ge(self.se.h, b_se0 + 2)
                    v.tensor_mul(
                        pbf.rearrange("p (a b) -> p a b", a=16),
                        pbf.rearrange("p (a b) -> p a b", a=16),
                        rec.rearrange("p (a b) -> p a b", a=16)[
                            :, :, 0:1].broadcast_to([128, 16, 32]))
                    self.ve.tick(v.transpose(pT, pbf))

                @blk.scalar
                def _(s, b_ve0=b_ve0):
                    s.wait_ge(self.ve.h, b_ve0 + 1)
                    self.se.tick(s.activation(pex, df, AF.Exp,
                                              scale=self.cst[:, 0:1]))
                    s.wait_ge(self.ve.h, b_ve0 + 18)
                    s.activation(rec, sums, AF.Ln)
                    self.se.tick(s.activation(rec, rec, AF.Exp,
                                              scale=self.cst[:, 2:3]))

                b_ve1 = self.ve.c
                b_pe2 = self.pe.c

                @blk.tensor
                def _(t, b_ve1=b_ve1):
                    t.wait_ge(self.ve.h, b_ve1)
                    last = None
                    for grp in range(16):
                        for j in range(4):
                            last = t.matmul(
                                self.bank(self.PSB, j)[
                                    :, grp * 32:(grp + 1) * 32],
                                vtv[32 * j:32 * (j + 1), grp, :],
                                pT[32 * j:32 * (j + 1),
                                   grp * 32:(grp + 1) * 32],
                                start=True, stop=True,
                                tile_position=(j * 32, 0))
                    self.pe.tick(last)

                b_ve2 = self.ve.c

                @blk.vector
                def _(v, tc=tc, g64=g64, b_pe2=b_pe2):
                    v.wait_ge(self.pe.h, b_pe2 + 1)
                    t0 = tc * TQ + g64 * 64
                    for j in range(4):
                        ab = self.bank(self.PSB, j)
                        dst = self.A()[:, :, t0 + j:t0 + 64:4].transpose(
                            [0, 2, 1])
                        self.ve.tick(v.tensor_copy(
                            dst, ab.rearrange("p (a b) -> p a b", a=16)))

                if _ATTNA_TAP and tc == 0 and g64 == 0:
                    @blk.scalar
                    def _(s, sb_=sb_):
                        s.wait_ge(self.ve.h, self.ve.c)
                        self.se.tick(s.activation(
                            self.S(29824, 1024, F32)[:, 0:512], sb_, AF.Copy))

                    @blk.gpsimd
                    def _(g):
                        g.wait_ge(self.se.h, self.se.c)
                        self.gd.tick(g.dma_start(
                            out=self.DBGA[:, 6, 0:512],
                            in_=self.S(29824, 1024, F32)[:, 0:512]), 16)

                    @blk.sync
                    def _(sy):
                        sy.wait_ge(self.ve.h, self.ve.c)
                        self.sd.tick(sy.dma_start(out=self.DBGA[:, 0, 0:512], in_=pm), 16)
                        self.sd.tick(sy.dma_start(out=self.DBGA[:, 1, 0:512], in_=pex), 16)
                        self.sd.tick(sy.dma_start(out=self.DBGA[:, 2, 0:512], in_=pbf), 16)
                        self.sd.tick(sy.dma_start(out=self.DBGA[:, 3, 0:512], in_=pT), 16)
                        self.sd.tick(sy.dma_start(out=self.DBGA[:, 4, :], in_=vt), 16)

                    @blk.gpsimd
                    def _(g):
                        g.wait_ge(self.ve.h, self.ve.c)
                        self.gd.tick(g.dma_start(out=self.DBGA[:, 5, 0:16], in_=sums), 16)
                        self.gd.tick(g.dma_start(out=self.DBGA[:, 5, 16:32], in_=rec), 16)

                self.barrier()

        self.barrier()

    def ph_proj_v_shared(self):
        blk = self.blk
        wb = self.S(0, 16384)
        est = [self.S(16384, 512), self.S(16896, 512)]
        b_gd, b_pe, b_se, b_sd = self.gd.c, self.pe.c, self.se.c, self.sd.c

        @blk.gpsimd
        def _(g):
            for dc in range(8):
                if dc >= 1:
                    g.wait_ge(self.pe.h, b_pe + dc * 16)
                self.gd.tick(g.dma_start(
                    out=wb.rearrange("p (i n) -> p i n", i=NCH),
                    in_=self.wbv[:, dc * 512:(dc + 1) * 512].rearrange(
                        "(i p) n -> p i n", p=128)), 16)

        @blk.tensor
        def _(t):
            for dc in range(8):
                t.wait_ge(self.gd.h, b_gd + 16 * (dc + 1))
                wv = wb.rearrange("p (i n) -> p i n", i=NCH)
                for tc in range(16):
                    k = dc * 16 + tc
                    ps = self.bank(self.PSA if k % 2 == 0 else self.PSB, 0)
                    if k >= 2:
                        t.wait_ge(self.se.h, b_se + (k - 1))
                    last = None
                    for i in range(NCH):
                        last = t.matmul(
                            ps, self.A(i)[:, tc * 128:(tc + 1) * 128],
                            wv[:, i, :], start=(i == 0), stop=(i == NCH - 1))
                    self.pe.tick(last)

        @blk.scalar
        def _(s):
            for dc in range(8):
                for tc in range(16):
                    k = dc * 16 + tc
                    s.wait_ge(self.pe.h, b_pe + k + 1)
                    if k >= 2:
                        s.wait_ge(self.sd.h, b_sd + 16 * (k - 1))
                    ps = self.bank(self.PSA if k % 2 == 0 else self.PSB, 0)
                    self.se.tick(s.activation(est[k % 2], ps, AF.Copy))

        @blk.sync
        def _(sy):
            for dc in range(8):
                for tc in range(16):
                    k = dc * 16 + tc
                    sy.wait_ge(self.se.h, b_se + k + 1)
                    dst = self.VL[4 * dc:4 * dc + 4, tc, :, :]
                    self.sd.tick(sy.dma_start(
                        out=dst.transpose([1, 0, 2]),
                        in_=est[k % 2].rearrange("p (h e) -> p h e", h=4)), 16)

        self.barrier()

    def ph_barrier_kv(self):
        blk = self.blk
        nc = self.nc

        @blk.gpsimd
        def _(g):
            pid = self.pid_of(g)
            self.gd.tick(g.dma_start(out=self.KT8[ds(pid, 1)][0],
                                     in_=self.KTL[:, :, 0:T]), 16)
            self.gd.tick(g.dma_start(out=self.V8[ds(pid, 1)][0],
                                     in_=self.VL[:, 0:16]), 16)
            g.wait_ge(self.gd.h, self.gd.c)
            self.gd.tick(g.dma_start(out=self.FLG[ds(pid, 1)][0],
                                     in_=self.nz_s[:]), 16)
            g.wait_ge(self.gd.h, self.gd.c)
            nonce_v = g.value_load(self.nz_s[0:1, 0:1])
            partner = pid // 2 * 2 + (1 - pid % 2)
            cnt = nc.alloc_registers("spincnt", engines=[ET.Pool])
            nc.regs_mov(cnt, 0)
            nc.br("kvspin", engines=[ET.Pool])
            with nc.body("kvspin", valid_engines=[ET.Pool]):
                nc.regs_alu(cnt, cnt, 16, op=ALU.add)
                cval = nc.snap(cnt, engines=[ET.Pool], min_val=16)
                g.dma_start(out=self.fr_s[:],
                            in_=self.FLG[ds(partner, 1)][0]).then_inc(self.sp.h, 16)
                g.wait_ge(self.sp.h, cval)
                v = g.value_load(self.fr_s[0:1, 0:1])
                nc.br_cmp(v, nonce_v, "kvdone", "kvspin", "IS_EQ",
                          engines=[ET.Pool])
            with nc.body("kvdone", valid_engines=[ET.Pool]):
                partner2 = pid // 2 * 2 + (1 - pid % 2)
                g.dma_start(out=self.KTL[:, :, T:2 * T],
                            in_=self.KT8[ds(partner2, 1)][0]).then_inc(self.gd.h, 16)
                g.dma_start(out=self.VL[:, 16:32],
                            in_=self.V8[ds(partner2, 1)][0]).then_inc(self.gd.h, 16)
                g.wait_ge(self.gd.h, self.gd.c + 32)
            self.gd.c += 32

        self.barrier()

    def ph_reload_A(self, src):
        blk = self.blk

        @blk.gpsimd
        def _(g):
            for kk in range(4):
                self.gd.tick(g.dma_start(
                    out=self.A()[:, kk * 8:(kk + 1) * 8, :],
                    in_=src[:, kk * 8:(kk + 1) * 8, :]), 16)

        self.barrier()

    def ph_attn_b(self):
        blk = self.blk
        kb = [self.S(0, 4096), self.S(4096, 4096)]
        vbr = [self.S(8192, 4096), self.S(12288, 4096)]
        pb = [self.S(16384, 2048), self.S(18432, 2048)]
        rb = self.S(20480, 1024, F32)
        avs = [self.S(21504, 512), self.S(22016, 512)]

        b_gd, b_pe, b_se, b_ve, b_sd = (self.gd.c, self.pe.c, self.se.c,
                                        self.ve.c, self.sd.c)
        PEH, SEH, VEH = 16, 10, 1

        @blk.gpsimd
        def _(g):
            for h in range(H):
                if h >= 2:
                    g.wait_ge(self.pe.h, b_pe + (h - 1) * NQ * PEH)
                self.gd.tick(g.dma_start(
                    out=kb[h % 2], in_=self.KTL[h]), 16)
                vv = vbr[h % 2].rearrange("p (a e) -> p a e", a=32)
                self.gd.tick(g.dma_start(
                    out=vv, in_=self.VL[h].transpose([1, 0, 2])), 16)

        @blk.tensor
        def _(t):
            for h in range(H):
                t.wait_ge(self.gd.h, b_gd + 32 * (h + 1))
                vv = vbr[h % 2].rearrange("p (a e) -> p a e", a=32)
                for q in range(NQ):
                    hq = h * NQ + q
                    avb = self.bank(self.PSB, hq % 2)
                    smb = self.bank(self.PSB, 2 + hq % 2)
                    if hq >= 2:
                        t.wait_ge(self.ve.h, b_ve + (hq - 2) * VEH + 1)
                    for kg in range(8):
                        if kg > 0:
                            t.wait_ge(self.se.h, b_se + hq * SEH + kg)
                        elif hq > 0:
                            t.wait_ge(self.se.h, b_se + (hq - 1) * SEH + 8)
                        last = None
                        for kk in range(4):
                            kt = kg * 4 + kk
                            last = t.matmul(
                                self.bank(self.PSA, kk),
                                kb[h % 2][:, kt * 128:(kt + 1) * 128],
                                self.A(h, q), start=True, stop=True)
                        self.pe.tick(last)
                        t.wait_ge(self.se.h, b_se + hq * SEH + kg + 1)
                        last = None
                        for kk in range(4):
                            kt = kg * 4 + kk
                            pbv = pb[kg % 2][:, kk * 512:(kk + 1) * 512]
                            t.matmul(smb, self.ones_sq[:], pbv,
                                     start=(kt == 0), stop=(kt == 31))
                            last = t.matmul(avb, vv[:, kt, :], pbv,
                                            start=(kt == 0), stop=(kt == 31))
                        self.pe.tick(last)

        @blk.scalar
        def _(s):
            for h in range(H):
                for q in range(NQ):
                    hq = h * NQ + q
                    smb = self.bank(self.PSB, 2 + hq % 2)
                    for kg in range(8):
                        s.wait_ge(self.pe.h, b_pe + hq * PEH + 2 * kg + 1)
                        if kg >= 2:
                            s.wait_ge(self.pe.h,
                                      b_pe + hq * PEH + 2 * (kg - 2) + 2)
                        elif hq > 0:
                            s.wait_ge(self.pe.h,
                                      b_pe + (hq - 1) * PEH + 2 * (kg + 6) + 2)
                        self.se.tick(s.activation(
                            pb[kg % 2], self.PSA[:, :], AF.Exp,
                            scale=self.cst[:, 0:1], bias=self.cst[:, 1:2]))
                    s.wait_ge(self.pe.h, b_pe + hq * PEH + 16)
                    if hq > 0:
                        s.wait_ge(self.ve.h, b_ve + (hq - 1) * VEH + 1)
                    self.se.tick(s.activation(rb, smb, AF.Ln))
                    self.se.tick(s.activation(rb, rb, AF.Exp,
                                              scale=self.cst[:, 2:3]))

        @blk.vector
        def _(v):
            for h in range(H):
                for q in range(NQ):
                    hq = h * NQ + q
                    avb = self.bank(self.PSB, hq % 2)
                    v.wait_ge(self.se.h, b_se + hq * SEH + 10)
                    if hq >= 2:
                        v.wait_ge(self.sd.h, b_sd + 16 * (hq - 1))
                    self.ve.tick(v.tensor_mul(avs[hq % 2], avb, rb))

        @blk.sync
        def _(sy):
            for h in range(H):
                for q in range(NQ):
                    hq = h * NQ + q
                    sy.wait_ge(self.ve.h, b_ve + hq * VEH + 1)
                    self.sd.tick(sy.dma_start(
                        out=self.SPAV2[:, h, q * QW:(q + 1) * QW],
                        in_=avs[hq % 2]), 16)

        self.barrier()

    def ph_ffn(self, xold, xnew, fin=False):
        blk = self.blk
        Lrows = self.S(0, 4096, F32, parts=24)
        HrowsR = self.S(0, 2048, BF16, parts=16)
        HrowsPre = self.S(2048, 2048, BF16, parts=4)
        HrowsPost = self.S(4096, 2048, BF16, parts=4)
        rho_b = self.S(16384, 4096, F32)
        hbv = self.S(12288, 3072)
        wbc = [self.S(18432, 1024), self.S(19456, 1024)]
        xpre = self.S(20480, 4096)
        ybf = self.S(24576, 4096)
        accf = self.S(28672, 1024, F32)
        tmpf = self.S(29696, 1024, F32)
        xof = self.S(30720, 1024, F32)

        hv = hbv.rearrange("p (r t) -> p r t", r=6)
        xpv = xpre.rearrange("p (c t) -> p c t", c=8)
        yv = ybf.rearrange("p (c t) -> p c t", c=8)

        b_pe = self.pe.c

        @blk.tensor
        def _(t):
            for q in range(NQ):
                last = None
                for i in range(NCH):
                    last = t.matmul(self.bank(self.PSA, q)[0:24, :],
                                    self.wcat_s[:, i, :], self.A(i, q),
                                    start=(i == 0), stop=(i == NCH - 1))
                self.pe.tick(last)

        b_pe1 = self.pe.c

        @blk.vector
        def _(v):
            v.wait_ge(self.pe.h, b_pe1)
            for q in range(NQ):
                v.tensor_mul(Lrows[:, q * QW:(q + 1) * QW],
                             self.bank(self.PSA, q)[0:24, :],
                             rho_b[0:24, q * QW:(q + 1) * QW])
            self.ve.tick(v.tensor_scalar_add(Lrows, Lrows, self.hb_s[:]))

        self.barrier()
        if _FFN_TAP:
            @blk.gpsimd
            def _(g):
                self.gd.tick(g.dma_start(out=self.DBGA[0:24, 7, :],
                                         in_=Lrows), 16)
                self.gd.tick(g.dma_start(out=self.DBGA[:, 12, 0:2048],
                                         in_=rho_b), 16)
            self.barrier()

        b_pe2 = self.pe.c
        b_ve2 = self.ve.c

        @blk.tensor
        def _(t):
            for tt in range(16):
                if tt >= 2:
                    t.wait_ge(self.ve.h, b_ve2 + 3 * (tt - 1))
                self.pe.tick(t.transpose(
                    self.bank(self.PSA, tt % 2)[0:128, 0:24],
                    Lrows[:, tt * 128:(tt + 1) * 128],
                    self.eye_f[0:24, 0:24]))

        @blk.vector
        def _(v):
            for tt in range(16):
                v.wait_ge(self.pe.h, b_pe2 + tt + 1)
                ps = self.bank(self.PSA, tt % 2)
                v.tensor_scalar_mul(self.Zres[:, tt, :], ps[:, 0:16], INV_TAU)
                v.tensor_copy(self.Zpre[:, tt, :], ps[:, 16:20])
                self.ve.tick(v.tensor_copy(self.Zpost[:, tt, :],
                                           ps[:, 20:24]), 3)

        self.barrier()

        b_se1 = self.se.c

        @blk.scalar
        def _(s):
            self.se.tick(s.activation(self.Hpre[:], self.Zpre[:], AF.Exp))
            self.se.tick(s.activation(self.Hpost[:], self.Zpost[:], AF.Exp))

        b_ve1b = self.ve.c

        @blk.vector
        def _(v):
            for k, hh in enumerate([self.Hpre, self.Hpost]):
                v.wait_ge(self.se.h, b_se1 + k + 1)
                self.ve.tick(v.tensor_reduce(self.sm4[:], hh[:], AX.X, ALU.add))
                v.wait_ge(self.se.h, b_se1 + 2 + 2 * (k + 1))
                self.ve.tick(v.tensor_mul(
                    hh[:], hh[:], self.sm4[:].broadcast_to([128, 16, 4])))

        @blk.scalar
        def _(s):
            for k in range(2):
                s.wait_ge(self.ve.h, b_ve1b + 2 * k + 1)
                s.activation(self.sm4[:], self.sm4[:], AF.Ln)
                self.se.tick(s.activation(self.sm4[:], self.sm4[:], AF.Exp,
                                          scale=self.cst[:, 2:3]), 2)

        @blk.vector
        def _(v):
            self.ve.tick(v.memset(self.lnSv[:], 0.0))

        Zv = self.Zres[:].rearrange("p a (i j) -> p a i j", i=4)
        Ev = self.Ebuf[:].rearrange("p a (i j) -> p a i j", i=4)
        EvT = self.Ebuf[:].rearrange("p a (i j) -> p a j i", i=4)
        for it in range(SINK_ITERS):
            b_v = self.ve.c
            b_s = self.se.c

            @blk.vector
            def _(v):
                v.tensor_sub(Ev, Zv, self.lnSv[:, :, None, :].broadcast_to(
                    [128, 16, 4, 4]))
                v.tensor_reduce(self.Mred[:, :, :, None], Ev, AX.X, ALU.max)
                self.ve.tick(v.tensor_sub(
                    Ev, Ev, self.Mred[:, :, :, None].broadcast_to(
                        [128, 16, 4, 4])))

            @blk.scalar
            def _(s, b_v=b_v):
                s.wait_ge(self.ve.h, b_v + 1)
                self.se.tick(s.activation(self.Ebuf[:], self.Ebuf[:], AF.Exp))

            @blk.vector
            def _(v, b_s=b_s):
                v.wait_ge(self.se.h, b_s + 1)
                self.ve.tick(v.tensor_reduce(self.Sred[:, :, :, None], Ev,
                                             AX.X, ALU.add))

            @blk.scalar
            def _(s, b_v=b_v):
                s.wait_ge(self.ve.h, b_v + 2)
                self.se.tick(s.activation(self.lnSu[:], self.Sred[:], AF.Ln))

            @blk.vector
            def _(v, b_s=b_s):
                v.wait_ge(self.se.h, b_s + 2)
                v.tensor_add(self.lnSu[:], self.lnSu[:], self.Mred[:])
                v.tensor_sub(Ev, Zv, self.lnSu[:, :, :, None].broadcast_to(
                    [128, 16, 4, 4]))
                v.tensor_reduce(self.Mred[:, :, :, None], EvT, AX.X, ALU.max)
                self.ve.tick(v.tensor_sub(
                    Ev, Ev, self.Mred[:, :, None, :].broadcast_to(
                        [128, 16, 4, 4])))

            @blk.scalar
            def _(s, b_v=b_v):
                s.wait_ge(self.ve.h, b_v + 3)
                self.se.tick(s.activation(self.Ebuf[:], self.Ebuf[:], AF.Exp))

            @blk.vector
            def _(v, b_s=b_s):
                v.wait_ge(self.se.h, b_s + 3)
                self.ve.tick(v.tensor_reduce(self.Sred[:, :, :, None], EvT,
                                             AX.X, ALU.add))

            @blk.scalar
            def _(s, b_v=b_v):
                s.wait_ge(self.ve.h, b_v + 4)
                self.se.tick(s.activation(self.lnSv[:], self.Sred[:], AF.Ln))

            @blk.vector
            def _(v, b_s=b_s):
                v.wait_ge(self.se.h, b_s + 4)
                self.ve.tick(v.tensor_add(self.lnSv[:], self.lnSv[:],
                                          self.Mred[:]))

        b_s2 = self.se.c
        b_v2 = self.ve.c

        @blk.vector
        def _(v):
            v.wait_ge(self.se.h, b_s2)
            v.tensor_sub(Ev, Zv,
                         self.lnSu[:, :, :, None].broadcast_to([128, 16, 4, 4]))
            self.ve.tick(v.tensor_sub(
                Ev, Ev, self.lnSv[:, :, None, :].broadcast_to([128, 16, 4, 4])))

        @blk.scalar
        def _(s):
            s.wait_ge(self.ve.h, b_v2 + 1)
            self.se.tick(s.activation(self.Hres[:], self.Ebuf[:], AF.Exp))

        Hv = self.Hres[:].rearrange("p a (i j) -> p a i j", i=4)

        @blk.vector
        def _(v):
            v.wait_ge(self.se.h, b_s2 + 1)
            v.tensor_scalar_mul(self.Hres[:], self.Hres[:], self.g_b[:, 0:1])
            self.ve.tick(v.scalar_tensor_tensor(
                Hv,
                self.eye4b[:].rearrange("p (i j) -> p i j", i=4)[
                    :, None, :, :].broadcast_to([128, 16, 4, 4]),
                self.g_b[:, 1:2], Hv, ALU.mult, ALU.add))

        self.barrier()

        b_pe3 = self.pe.c
        b_ve3 = self.ve.c

        @blk.tensor
        def _(t):
            for tt in range(16):
                if tt >= 2:
                    t.wait_ge(self.ve.h, b_ve3 + 3 * (tt - 1))
                self.pe.tick(t.transpose(
                    self.bank(self.PSA, tt % 2)[0:16, 0:128],
                    self.Hres[:, tt, :], self.eye_f[:]))
                self.pe.tick(t.transpose(
                    self.bank(self.PSB, tt % 2)[0:4, 0:128],
                    self.Hpre[:, tt, :], self.eye_f[:]))
                self.pe.tick(t.transpose(
                    self.bank(self.PSB, 2 + tt % 2)[0:4, 0:128],
                    self.Hpost[:, tt, :], self.eye_f[:]))

        @blk.vector
        def _(v):
            for tt in range(16):
                v.wait_ge(self.pe.h, b_pe3 + 3 * tt + 3)
                sl = slice(tt * 128, (tt + 1) * 128)
                v.tensor_copy(HrowsR[0:16, sl],
                              self.bank(self.PSA, tt % 2)[0:16, 0:128])
                v.tensor_copy(HrowsPre[0:4, sl],
                              self.bank(self.PSB, tt % 2)[0:4, 0:128])
                self.ve.tick(v.tensor_copy(
                    HrowsPost[0:4, sl],
                    self.bank(self.PSB, 2 + tt % 2)[0:4, 0:128]), 3)

        self.barrier()
        if _FFN_TAP:
            @blk.gpsimd
            def _(g):
                self.gd.tick(g.dma_start(out=self.DBGA[:, 8, 0:256],
                                         in_=self.Hres[:].rearrange(
                                             "p a b -> p (a b)")), 16)
                self.gd.tick(g.dma_start(out=self.DBGA[:, 9, 0:64],
                                         in_=self.Hpre[:].rearrange(
                                             "p a b -> p (a b)")), 16)
                self.gd.tick(g.dma_start(out=self.DBGA[:, 9, 64:128],
                                         in_=self.Hpost[:].rearrange(
                                             "p a b -> p (a b)")), 16)
                self.gd.tick(g.dma_start(out=self.DBGA[0:16, 10, :],
                                         in_=HrowsR), 16)
                self.gd.tick(g.dma_start(out=self.DBGA[0:4, 11, 0:2048],
                                         in_=HrowsPre), 16)
                self.gd.tick(g.dma_start(out=self.DBGA[0:4, 11, 2048:4096] if False else self.DBGA[4:8, 11, 0:2048],
                                         in_=HrowsPost), 16)
            self.barrier()

        def selT(nrows, r):
            return self.eye_bf[0:nrows, r:r + 1].broadcast_to([nrows, 128])

        for q in range(NQ):
            qs = slice(q * QW, (q + 1) * QW)
            b_pe4 = self.pe.c
            b_se4 = self.se.c

            @blk.tensor
            def _(t, qs=qs):
                for j in range(4):
                    self.pe.tick(t.matmul(
                        self.bank(self.PSA, j), selT(4, j),
                        HrowsPre[0:4, qs], start=True, stop=True))

            @blk.scalar
            def _(s, b_pe4=b_pe4):
                for j in range(4):
                    s.wait_ge(self.pe.h, b_pe4 + j + 1)
                    self.se.tick(s.activation(hv[:, j, :],
                                              self.bank(self.PSA, j), AF.Copy))

            b_se5 = self.se.c
            b_ve5 = self.ve.c

            @blk.vector
            def _(v, q=q, b_se5=b_se5):
                v.wait_ge(self.se.h, b_se5)
                for c in range(8):
                    v.tensor_mul(accf, self.A(0 * 8 + c, q), hv[:, 0, :])
                    for n in range(1, 4):
                        v.tensor_mul(tmpf, self.A(n * 8 + c, q), hv[:, n, :])
                        v.tensor_add(accf, accf, tmpf)
                    self.ve.tick(v.tensor_copy(xpv[:, c, :], accf))

            b_gd5 = self.gd.c
            b_pe5 = self.pe.c

            @blk.gpsimd
            def _(g, b_pe5=b_pe5):
                for cp in range(8):
                    if cp >= 2:
                        g.wait_ge(self.pe.h, b_pe5 + cp - 1)
                    self.gd.tick(g.dma_start(
                        out=wbc[cp % 2].rearrange("p (c n) -> p c n", c=8),
                        in_=self.wbr[:, cp * 128:(cp + 1) * 128].rearrange(
                            "(c p) n -> p c n", p=128)), 16)

            @blk.tensor
            def _(t, b_ve5=b_ve5, b_gd5=b_gd5):
                t.wait_ge(self.ve.h, b_ve5 + 8)
                for cp in range(8):
                    t.wait_ge(self.gd.h, b_gd5 + 16 * (cp + 1))
                    ps = self.bank(self.PSB, cp % 4)
                    wv2 = wbc[cp % 2].rearrange("p (c n) -> p c n", c=8)
                    last = None
                    for c in range(8):
                        last = t.matmul(ps, wv2[:, c, :], xpv[:, c, :],
                                        start=(c == 0), stop=(c == 7))
                    self.pe.tick(last)

            b_se6 = self.se.c

            @blk.scalar
            def _(s, b_pe5=b_pe5):
                for cp in range(8):
                    s.wait_ge(self.pe.h, b_pe5 + cp + 1)
                    self.se.tick(s.activation(yv[:, cp, :],
                                              self.bank(self.PSB, cp % 4),
                                              AF.Copy))

            if _FFN_TAP and q == 0:
                @blk.gpsimd
                def _(g, b_se6=b_se6):
                    g.wait_ge(self.se.h, b_se6 + 8)
                    self.gd.tick(g.dma_start(out=self.DBGA[:, 13, :],
                                             in_=xpre[:, 0:2048]), 16)
                    self.gd.tick(g.dma_start(out=self.DBGA[:, 14, :],
                                             in_=xpre[:, 2048:4096]), 16)
                    self.gd.tick(g.dma_start(out=self.DBGA[:, 15, :],
                                             in_=ybf[:, 0:2048]), 16)
                    self.gd.tick(g.dma_start(out=self.DBGA[:, 16, :],
                                             in_=ybf[:, 2048:4096]), 16)
                self.barrier()

            for n in range(4):
                b_pe6 = self.pe.c
                b_se7 = self.se.c

                @blk.tensor
                def _(t, n=n, qs=qs, b_se7=b_se7):
                    t.wait_ge(self.se.h, b_se7)
                    for j in range(4):
                        self.pe.tick(t.matmul(
                            self.bank(self.PSA, j), selT(16, n * 4 + j),
                            HrowsR[0:16, qs], start=True, stop=True))
                    self.pe.tick(t.matmul(
                        self.bank(self.PSB, 0), selT(4, n),
                        HrowsPost[0:4, qs], start=True, stop=True))

                b_ve7 = self.ve.c

                @blk.scalar
                def _(s, n=n, b_pe6=b_pe6, b_ve7=b_ve7):
                    if n > 0:
                        s.wait_ge(self.ve.h, b_ve7)
                    for j in range(4):
                        s.wait_ge(self.pe.h, b_pe6 + j + 1)
                        self.se.tick(s.activation(hv[:, j, :],
                                                  self.bank(self.PSA, j),
                                                  AF.Copy))
                    s.wait_ge(self.pe.h, b_pe6 + 5)
                    self.se.tick(s.activation(hv[:, 4, :],
                                              self.bank(self.PSB, 0), AF.Copy))

                b_se8 = self.se.c
                b_sd8 = self.sd.c
                b_ve8 = self.ve.c

                @blk.sync
                def _(sy, n=n, q=q, qs=qs, b_ve8=b_ve8):
                    for c in range(8):
                        if c >= 1:
                            sy.wait_ge(self.ve.h, b_ve8 + c)
                        self.sd.tick(sy.dma_start(
                            out=xof, in_=xold[:, n * 8 + c, qs]), 16)
                        sy.wait_ge(self.ve.h, b_ve8 + c + 1)
                        dst = (self.OUT.rearrange("(i p) t -> p i t", p=128)
                               if fin else xnew)
                        self.sd.tick(sy.dma_start(
                            out=dst[:, n * 8 + c, qs], in_=accf), 16)

                @blk.vector
                def _(v, n=n, q=q, b_se8=b_se8, b_sd8=b_sd8):
                    for c in range(8):
                        v.wait_ge(self.se.h, b_se8)
                        v.wait_ge(self.sd.h, b_sd8 + 16 * (2 * c + 1))
                        v.tensor_mul(accf, self.A(0 * 8 + c, q), hv[:, 0, :])
                        for j in range(1, 4):
                            v.tensor_mul(tmpf, self.A(j * 8 + c, q),
                                         hv[:, j, :])
                            v.tensor_add(accf, accf, tmpf)
                        v.tensor_mul(tmpf, yv[:, c, :], hv[:, 4, :])
                        v.tensor_add(accf, accf, tmpf)
                        self.ve.tick(v.tensor_add(accf, accf, xof))

            self.barrier()

    def ph_final(self):
        blk = self.blk
        sd_c = self.sd.c
        gd_c = self.gd.c

        @blk.sync
        def _(sy):
            sy.wait_ge(self.sd.h, sd_c)

        @blk.gpsimd
        def _(g):
            g.wait_ge(self.gd.h, gd_c)


def build_program(upto=99, dump_a_after=None):
    b = B()

    def dump_a():
        blk = b.blk

        @blk.sync
        def _(sy):
            for kk in range(4):
                b.sd.tick(sy.dma_start(
                    out=b.DBGA[:, kk * 8:(kk + 1) * 8, :],
                    in_=b.A()[:, kk * 8:(kk + 1) * 8, :]), 16)
        b.barrier()

    steps = []
    steps.append(lambda: b.ph_init())
    steps.append(lambda: b.ph_stats(None, need_rho=False))
    steps.append(lambda: b.ph_apply_ln(None))
    steps.append(lambda: b.ph_proj_fm(b.waq, lambda d, e, sy, ctx: b.sd.tick(
        sy.dma_start(out=b.SPQ[:, d, :], in_=e), 16),
        bias_col=lambda d: b.ba_s[:, d:d + 1]))
    steps.append(lambda: b.ph_proj_fm(b.wak, lambda d, e, sy, ctx: b.sd.tick(
        sy.dma_start(out=b.SPK[:, d, :], in_=e), 16),
        bias_col=lambda d: b.ba_s[:, 32 + d:32 + d + 1]))
    steps.append(lambda: b.ph_proj_fm(b.wav, lambda d, e, sy, ctx: b.sd.tick(
        sy.dma_start(out=b.SPV[:, d, :], in_=e), 16),
        bias_col=lambda d: b.ba_s[:, 64 + d:64 + d + 1]))
    steps.append(lambda: b.ph_attn_a())
    steps.append(lambda: b.ph_proj_resid(
        b.wao, b.xT.rearrange("(i p) t -> p i t", p=128), b.SPX1,
        bias_col=lambda d: b.ba_s[:, 96 + d:96 + d + 1]))
    steps.append(lambda: b.ph_stats(b.SPX1, need_rho=True))
    steps.append(lambda: b.ph_apply_ln(b.SPX1))
    steps.append(lambda: b.ph_ffn(b.SPX1, b.SPX2))
    steps.append(lambda: b.ph_stats(b.SPX2, need_rho=False))
    steps.append(lambda: b.ph_apply_ln(b.SPX2))
    steps.append(lambda: b.ph_proj_fm(b.wbk, lambda d, e, sy, ctx: b.sd.tick(
        sy.dma_start(out=b.KTL[d][:, 0:T], in_=e), 16)))
    steps.append(lambda: b.ph_proj_v_shared())
    steps.append(lambda: b.ph_proj_fm(b.wbq, lambda d, e, sy, ctx: b.sd.tick(
        sy.dma_start(out=b.SPQ2[:, d, :], in_=e), 16)))
    steps.append(lambda: b.ph_barrier_kv())
    steps.append(lambda: b.ph_reload_A(b.SPQ2))
    steps.append(lambda: b.ph_attn_b())
    steps.append(lambda: b.ph_reload_A(b.SPAV2))
    steps.append(lambda: b.ph_proj_resid(b.wbo, b.SPX2, b.SPX3))
    steps.append(lambda: b.ph_stats(b.SPX3, need_rho=True))
    steps.append(lambda: b.ph_apply_ln(b.SPX3))
    steps.append(lambda: b.ph_ffn(b.SPX3, None, fin=True))
    for i, s in enumerate(steps[:upto]):
        s()
        if dump_a_after is not None and i + 1 == dump_a_after:
            dump_a()
    b.ph_final()
    return b


def _host_inputs(inputs):
    x = np.asarray(inputs["L0_attn_a_x"])
    f = np.float32
    wT = {
        "waq": np.ascontiguousarray(np.asarray(inputs["L0_attn_a_Wq"]).T),
        "wak": np.ascontiguousarray(np.asarray(inputs["L0_attn_a_Wk"]).T),
        "wav": np.ascontiguousarray(np.asarray(inputs["L0_attn_a_Wv"]).T),
        "wao": np.ascontiguousarray(np.asarray(inputs["L0_attn_a_Wo"]).T),
        "wbq": np.ascontiguousarray(np.asarray(inputs["L0_attn_b_w_q"]).T),
        "wbk": np.ascontiguousarray(np.asarray(inputs["L0_attn_b_w_k"]).T),
        "wbv": np.ascontiguousarray(np.asarray(inputs["L0_attn_b_w_v"]).T),
        "wbo": np.ascontiguousarray(np.asarray(inputs["L0_attn_b_w_o"]).T),
    }
    wcat = np.ascontiguousarray(np.concatenate(
        [np.asarray(inputs["L0_ffn_W_res"]), np.asarray(inputs["L0_ffn_W_pre"]),
         np.asarray(inputs["L0_ffn_W_post"])], axis=0).T)
    wbr = np.ascontiguousarray(np.asarray(inputs["L0_ffn_W_branch"]))
    blay = {
        "b_aq": np.ascontiguousarray(
            np.asarray(inputs["L0_attn_a_bq"]).reshape(32, 128).T),
        "b_ak": np.ascontiguousarray(
            np.asarray(inputs["L0_attn_a_bk"]).reshape(32, 128).T),
        "b_av": np.ascontiguousarray(
            np.asarray(inputs["L0_attn_a_bv"]).reshape(32, 128).T),
        "b_ao": np.ascontiguousarray(
            np.asarray(inputs["L0_attn_a_bo"]).reshape(32, 128).T),
    }
    hbase = np.ascontiguousarray(np.concatenate(
        [np.asarray(inputs["L0_ffn_H_res_base"]).reshape(-1),
         np.asarray(inputs["L0_ffn_H_pre_base"]).reshape(-1),
         np.asarray(inputs["L0_ffn_H_post_base"]).reshape(-1)]).reshape(24, 1))
    gate = np.asarray(inputs["L0_ffn_gate"]).reshape(1, 1).astype(f)
    eye = np.eye(128, dtype=f)
    eye4 = np.ascontiguousarray(np.eye(4, dtype=f).reshape(1, 16))
    nonce = np.full((1, 8), int(np.random.randint(1, 2 ** 30)), np.int32)

    in_maps = []
    for c in range(N_CORE):
        bb, hf = c // 2, c % 2
        m = {"xT": np.ascontiguousarray(
            np.asarray(x[bb, hf * T:(hf + 1) * T, :]).T.astype(f))}
        m.update(wT)
        m.update(blay)
        m.update(dict(wcat=wcat, wbr=wbr, hbase=hbase, gate=gate, eye=eye,
                      eye4=eye4, nonce=nonce,
                      cid=np.full((1, 8), c, np.int32)))
        in_maps.append(m)
    return in_maps


def kernel(**inputs):
    global _LAST
    if "nc" not in _CACHE:
        _CACHE["nc"] = build_program()
    b = _CACHE["nc"]
    in_maps = _host_inputs(inputs)
    res = run_bass_kernel_spmd(b.nc, in_maps, core_ids=list(range(N_CORE)))
    _LAST = res
    out = np.empty((4, D, D), np.float32)
    for c in range(N_CORE):
        bb, hf = c // 2, c % 2
        out[bb, hf * T:(hf + 1) * T, :] = np.asarray(res.results[c]["OUT"]).T
    return out
